# version 09 (final)
# baseline (speedup 1.0000x reference)
import contextlib
import ml_dtypes
import numpy as np

import concourse.bass as bass
import concourse.mybir as mybir
from concourse.bass_utils import run_bass_kernel_spmd

F32 = mybir.dt.float32
BF16 = mybir.dt.bfloat16
U8 = mybir.dt.uint8
ALU = mybir.AluOpType
AF = mybir.ActivationFunctionType
AX = mybir.AxisListType

H, P, N, G, KC = 64, 64, 128, 8, 4
DM = 4096
I = H * P
GN = G * N
CONV_DIM = I + 2 * GN
PROJ = I + CONV_DIM + H
EPS = 1e-5
QCH = 128
NCT = 81
NDT = 32
NIT = 32
NXT = 48
HPG = H // G


def _host_prepare(inputs, TOK, NCORES):
    x = np.ascontiguousarray(inputs["L0_attn_hidden_states"], dtype=np.float32)
    Bsz, L, _ = x.shape
    w_in = inputs["L0_attn_in_proj_w"].astype(np.float32)
    w_out = inputs["L0_attn_out_proj_w"].astype(np.float32)
    qw = inputs["L0_ffn_qweight"]
    lut4 = inputs["L0_ffn_lut4"].astype(np.float32)

    CTS = -(-NCT // NCORES)
    OPC = DM // NCORES
    NOT = OPC // 128
    ORC = DM // NCORES

    wiT = np.zeros((NCT * 128, DM), np.float32)
    wiT[:PROJ, :] = w_in
    t = wiT.reshape(NCT, 128, NDT, 128)
    wi_tiled = np.ascontiguousarray(t.transpose(0, 3, 2, 1))
    wi_shards = []
    for c in range(NCORES):
        sh = np.zeros((CTS, 128, NDT, 128), np.float32)
        lo, hi = CTS * c, min(CTS * c + CTS, NCT)
        if hi > lo:
            sh[:hi - lo] = wi_tiled[lo:hi]
        wi_shards.append(sh)

    woT = np.ascontiguousarray(w_out.T)
    wo_shards = [np.ascontiguousarray(woT[c * ORC:(c + 1) * ORC])
                 for c in range(NCORES)]

    shifts = np.arange(32, dtype=np.int32)
    bit_shards, lut_shards = [], []
    for c in range(NCORES):
        sl = slice(c * OPC, (c + 1) * OPC)
        bs = np.zeros((NOT, 128, 4, DM), np.uint8)
        for p in range(4):
            b = ((qw[p][sl][:, :, None] >> shifts) & 1).astype(np.uint8)
            bs[:, :, p, :] = b.reshape(NOT, 128, DM)
        bit_shards.append(np.ascontiguousarray(bs))
        lut_shards.append(np.ascontiguousarray(
            lut4[sl].reshape(NOT, 128, 16).transpose(1, 0, 2)))

    x_shards, xh_shards = [], []
    for c in range(NCORES):
        b, hh = divmod(c, 2)
        t0 = b * L + hh * TOK
        x_shards.append(np.ascontiguousarray(x.reshape(Bsz * L, DM)[t0:t0 + TOK]))
        xh = np.zeros((128, DM), np.float32)
        if hh == 1:
            xh[0:3] = x.reshape(Bsz * L, DM)[t0 - 3:t0]
        xh_shards.append(xh)

    dvec = inputs["L0_attn_D"].astype(np.float32)
    d_tiled = np.zeros((128, NIT), np.float32)
    for it in range(NIT):
        d_tiled[0:64, it] = dvec[2 * it]
        d_tiled[64:128, it] = dvec[2 * it + 1]
    gnsel = np.zeros((128, NIT, 8), np.float32)
    gnselt = np.zeros((8, NIT, 128), np.float32)
    for it in range(NIT):
        gnsel[:, it, it // 4] = 1.0
        gnselt[it // 4, it, :] = 1.0

    consts = dict(
        conv_w=np.ascontiguousarray(
            inputs["L0_attn_conv_weight"].astype(np.float32).reshape(NXT, 128, KC)
            .transpose(1, 0, 2)),
        conv_b=np.ascontiguousarray(
            inputs["L0_attn_conv_bias"].astype(np.float32).reshape(NXT, 128).T),
        dt_bias=inputs["L0_attn_dt_bias"].astype(np.float32).reshape(64, 1),
        a_log=inputs["L0_attn_A_log"].astype(np.float32).reshape(64, 1),
        d_tiled=d_tiled,
        nw=np.ascontiguousarray(
            inputs["L0_attn_norm_weight"].astype(np.float32).reshape(NIT, 128).T),
        ffn_bias=inputs["L0_ffn_bias"].astype(np.float32).reshape(1, DM),
        ident=np.eye(128, dtype=np.float32),
        tri01=np.triu(np.ones((128, 128), np.float32)),
        gnsel=gnsel,
        gnselt=gnselt.astype(ml_dtypes.bfloat16),
    )
    in_maps = []
    for c in range(NCORES):
        in_maps.append(dict(
            x_in=x_shards[c], xhalo=xh_shards[c], wi_sl=wi_shards[c],
            wo_sl=wo_shards[c], bits=bit_shards[c], lut4=lut_shards[c],
            st_scale=np.full((128, 1), 1.0 if (c % 2) else 0.0, np.float32),
            **consts))
    return in_maps


def build_program(TOK, NCORES):
    import os
    PH = int(os.environ.get("KPH", "99"))
    nc = bass.Bass("TRN2", target_bir_lowering=False)
    NCH = TOK // QCH
    NTT = TOK // 128
    TOKH = TOK // 2
    NTC = max(1, TOKH // 512)
    TCH = TOKH // NTC
    NTTH = TOKH // 128
    CTS = -(-NCT // NCORES)
    OPC = DM // NCORES
    NOT = OPC // 128
    ORC = DM // NCORES
    NWOC = ORC // 128
    HCH = H * NCH
    ISTRIP = 1024
    NSTRIP = DM // ISTRIP
    NWOSL = I // 512

    x_in = nc.dram_tensor("x_in", [TOK, DM], F32, kind="ExternalInput").ap()
    xhalo = nc.dram_tensor("xhalo", [128, DM], F32, kind="ExternalInput").ap()
    wi_sl = nc.dram_tensor("wi_sl", [CTS, 128, NDT, 128], F32,
                           kind="ExternalInput").ap()
    wo_sl = nc.dram_tensor("wo_sl", [ORC, DM], F32, kind="ExternalInput").ap()
    bits_in = nc.dram_tensor("bits", [NOT, 128, 4, DM], U8,
                             kind="ExternalInput").ap()
    lut4_in = nc.dram_tensor("lut4", [128, NOT, 16], F32, kind="ExternalInput").ap()
    convw_in = nc.dram_tensor("conv_w", [128, NXT, KC], F32,
                              kind="ExternalInput").ap()
    convb_in = nc.dram_tensor("conv_b", [128, NXT], F32, kind="ExternalInput").ap()
    dtb_in = nc.dram_tensor("dt_bias", [64, 1], F32, kind="ExternalInput").ap()
    alog_in = nc.dram_tensor("a_log", [64, 1], F32, kind="ExternalInput").ap()
    dtl_in = nc.dram_tensor("d_tiled", [128, NIT], F32, kind="ExternalInput").ap()
    nw_in = nc.dram_tensor("nw", [128, NIT], F32, kind="ExternalInput").ap()
    fb_in = nc.dram_tensor("ffn_bias", [1, DM], F32, kind="ExternalInput").ap()
    id_in = nc.dram_tensor("ident", [128, 128], F32, kind="ExternalInput").ap()
    tri_in = nc.dram_tensor("tri01", [128, 128], F32, kind="ExternalInput").ap()
    gne_in = nc.dram_tensor("gnsel", [128, NIT, 8], F32, kind="ExternalInput").ap()
    gnt_in = nc.dram_tensor("gnselt", [8, NIT, 128], BF16,
                            kind="ExternalInput").ap()
    ssc_in = nc.dram_tensor("st_scale", [128, 1], F32, kind="ExternalInput").ap()

    out = nc.dram_tensor("out", [TOK, DM], F32, kind="ExternalOutput").ap()

    wiT_in = nc.dram_tensor("wiT_in", [CTS, 128, NDT, 128], BF16,
                            kind="Internal").ap()
    wiT_ag = nc.dram_tensor("wiT_ag", [NCORES * CTS, 128, NDT, 128], BF16,
                            kind="Internal").ap()
    woT_in = nc.dram_tensor("woT_in", [ORC, DM], BF16, kind="Internal").ap()
    woT_ag = nc.dram_tensor("woT_ag", [I, DM], BF16, kind="Internal").ap()
    wlT_in = nc.dram_tensor("wlT_in", [NIT, 128, OPC], BF16, kind="Internal").ap()
    wlT_ag = nc.dram_tensor("wlT_ag", [NCORES, NIT, 128, OPC], BF16,
                            kind="Internal").ap()
    st_in = nc.dram_tensor("st_in", [128, H * 64], F32, kind="Internal").ap()
    lfl_d = nc.dram_tensor("lfl_d", [64, TOK], F32, kind="Internal").ap()
    cfl_d = nc.dram_tensor("cfl_d", [64, TOK], F32, kind="Internal").ap()
    st_ag = nc.dram_tensor("st_ag", [2, 128, H * 64], F32, kind="Internal").ap()

    xbc_sp = nc.dram_tensor("xbc_sp", [NCH, NXT, 128, QCH], BF16,
                            kind="Internal").ap()
    gate_sp = nc.dram_tensor("gate_sp", [NCH, NIT, 128, QCH], BF16,
                             kind="Internal").ap()
    yn_sp = nc.dram_tensor("yn_sp", [NCH, NIT, 128, QCH], BF16,
                           kind="Internal").ap()
    mix_sp = nc.dram_tensor("mix_sp", [NTT, 128, DM], BF16, kind="Internal").ap()
    x2_sp = nc.dram_tensor("x2_sp", [NTT, 128, DM], F32, kind="Internal").ap()
    xf_sp = nc.dram_tensor("xf_sp", [TOK // 128, NDT, 128, 128], BF16,
                           kind="Internal").ap()
    yf_sp = nc.dram_tensor("yf_sp", [NTT, 128, DM], BF16, kind="Internal").ap()

    es = contextlib.ExitStack()
    sb = lambda n, s, d: es.enter_context(nc.sbuf_tensor(n, s, d))
    ps = lambda n, s, d=F32: es.enter_context(nc.psum_tensor(n, s, d))

    xnfm = sb("xnfm", [128, NDT, TOKH + 4], BF16)
    id_b = sb("id_b", [128, 128], BF16)
    id_f = sb("id_f", [128, 128], F32)
    tri_b = sb("tri_b", [128, 128], BF16)
    gne_b = sb("gne_b", [128, NIT, 8], BF16)
    convw_s = sb("convw_s", [128, NXT, KC], F32)
    convb_s = sb("convb_s", [128, NXT], F32)
    dtb_s = sb("dtb_s", [64, 1], F32)
    alog_s = sb("alog_s", [64, 1], F32)
    nega_s = sb("nega_s", [64, 1], F32)
    dtl_s = sb("dtl_s", [128, NIT], F32)
    nw_s = sb("nw_s", [128, NIT], F32)
    ssc_s = sb("ssc_s", [128, 1], F32)
    ones1f = sb("ones1f", [1, 128], F32)
    ones65 = sb("ones65", [65, 128], F32)
    zer128 = sb("zer128", [128, 128], F32)
    dt_fm = sb("dt_fm", [64, TOK], F32)
    lloc = sb("lloc", [64, TOK], F32)
    cum_a = sb("cum_a", [64, TOK], F32)
    rev_a = sb("rev_a", [64, TOK], F32)
    dect = sb("dect", [64, NCH], F32)
    dkfl = sb("dkfl", [1, HCH], F32)
    dkbc = sb("dkbc", [128, HCH], F32)
    s_f = sb("s_f", [128, H * 64], F32)
    st1 = sb("st1", [128, 12], F32)

    pA = [ps(f"pA{i}", [128, 512]) for i in range(2)]
    pb4 = ps("pb4", [128, 512])
    pb5 = ps("pb5", [128, 512])
    pb7 = ps("pb7", [128, 512])
    pb8 = ps("pb8", [128, 512])
    pbB = ps("pbB", [128, 1024], BF16)
    pbB2 = ps("pbB2", [128, 1024], BF16)
    p_tf = [pb7[:, 0:128], pb7[:, 128:256]]
    p_g = pA[0][:, 128:256]
    p_h3 = pb4[:, 320:328]
    p_gn = pA[1][:, 128:136]
    p_lb = [pb4[:, 0:128], pb5[:, 0:128]]
    p_cu = [pb4[:, 128:256], pb5[:, 128:256]]
    p_y = [pA[0][:, 0:128], pA[1][:, 0:128]]
    p_rs = [pb7[:, 256:384], pb8[:, 0:128]]
    p_x = [pb4[:, 256:320], pb5[:, 256:320]]
    p_tb = [pbB[:, 0:128], pbB2[:, 0:128]]
    p_tc = [pbB[:, 128:256], pbB2[:, 128:256]]

    class Sem:
        __slots__ = ("h", "n")
        def __init__(self, name):
            self.h = es.enter_context(nc.semaphore(name))
            self.n = 0
    S = {}
    for nm in ("ld0 gq0 cld0 cdone wb0 wbl agr bld0 bitc deqv tp ev ccs ph "
               "vc xld0 xld1 lnv lns lna wld0 wld1 mmr evc scg scx cvv spg "
               "spx dtq dts lf0 xrl0 xrl1 tp2 ue2 vb2 pex st2 cq sbc gev "
               "dsv exq mtq cpq yev sqq gnm rsq ynd wol0 wol1 opd "
               "mxs spm ynl0 ynl1 mxl x2s xfs ffl0 ffl1 ffd yfs yfl outd "
               "grl gld0 gld1 ue2b vb2b ut2 tpf tpb rsm spy ymd dsv2 "
               "scT sti spg1 spx1 spm1 od0 od1 xfs2 yfl1 fbc").split():
        S[nm] = Sem(nm)

    def INC(instr, nm, k=1):
        instr.then_inc(S[nm].h, k)
        S[nm].n += k
        return instr

    def W(eng, nm, val):
        eng.wait_ge(S[nm].h, val)

    for nm in ("arr_sync", "arr_vector", "arr_scalar", "arr_tensor", "go"):
        S[nm] = Sem(nm)

    block = es.enter_context(nc.Block())
    bark = {"sync": 0, "vector": 0, "scalar": 0, "gpsimd": 0, "tensor": 0}

    def bar(eng, name):
        bark[name] += 1
        k = bark[name]
        if name == "gpsimd":
            for other in ("sync", "vector", "scalar", "tensor"):
                eng.wait_ge(S["arr_" + other].h, k)
            eng.sem_inc(S["go"].h, 1)
        else:
            eng.sem_inc(S["arr_" + name].h, 1)
            eng.wait_ge(S["go"].h, k)

    def _entry(eng):
        for nm, s in S.items():
            if s.n > 0 and nm not in ("ph", "go") and not nm.startswith("arr_"):
                eng.wait_ge(s.h, s.n)

    def entry_all():
        @block.sync
        def _(sq):
            _entry(sq)
        @block.vector
        def _(v):
            _entry(v)
        @block.scalar
        def _(sc):
            _entry(sc)
        @block.tensor
        def _(te):
            _entry(te)
        @block.gpsimd
        def _(gq):
            _entry(gq)

    if PH <= -1:
        es.close()
        return nc
    KP0 = int(os.environ.get("KP0", "4"))
    KP3 = int(os.environ.get("KP3", "4"))
    KP2 = int(os.environ.get("KP2", "2"))
    KNEU = os.environ.get("KNEU", "")
    KNR = os.environ.get("KNR", "")
    NCAST = CTS + NWOC
    NITS = ISTRIP // 128

    with contextlib.ExitStack() as px:
        sbl = lambda n, s, d: px.enter_context(nc.sbuf_tensor(n, s, d))
        c_f = sbl("c_f", [128, NDT * 128], F32)
        c_b = sbl("c_b", [128, NDT * 128], BF16)
        lut4_t = sbl("lut4_t", [128, NOT, 16], F32)
        lutd = sbl("lutd", [128, NOT, 8], F32)
        bitu = sbl("bitu", [128, 4, DM], U8)
        bitb = sbl("bitb", [128, 1, ISTRIP], BF16)
        arm = [sbl(f"arm{i}", [128, ISTRIP], BF16) for i in range(8)]
        gne_f = sbl("gne_f", [128, NIT, 8], F32)
        tri_f = sbl("tri_f", [128, 128], F32)
        wlT_sb = sbl("wlT_sb", [128, NIT, 128], BF16)

        @block.sync
        def _(sq):
            for dst, src in [(lut4_t[:], lut4_in), (convw_s[:], convw_in),
                             (convb_s[:], convb_in), (dtb_s[:], dtb_in),
                             (alog_s[:], alog_in), (dtl_s[:], dtl_in),
                             (nw_s[:], nw_in),
                             (id_f[:], id_in), (tri_f[:], tri_in),
                             (gne_f[:], gne_in),
                             (ssc_s[:], ssc_in)]:
                INC(sq.dma_start(out=dst, in_=src), "ld0", 16)
            if KP0 >= 2:
                for j in range(NCAST):
                    if j >= 1:
                        W(sq, "cdone", j)
                    src = (wi_sl[j] if j < CTS
                           else wo_sl[(j - CTS) * 128:(j - CTS + 1) * 128])
                    INC(sq.dma_start(out=c_f[:], in_=src), "cld0", 16)
            if KP0 >= 3:
                for ot in range(NOT):
                    if ot >= 1 and KP3 >= 2:
                        W(sq, "bitc", ot * NSTRIP)
                    INC(sq.dma_start(out=bitu[:], in_=bits_in[ot]), "bld0", 16)
            bar(sq, "sync")

        @block.gpsimd
        def _(gq):
            INC(gq.memset(zer128[:], 0.0), "gq0")
            INC(gq.memset(ones1f[:], 1.0), "gq0")
            gq.memset(ones65[:], 1.0)
            INC(gq.memset(s_f[:], 0.0), "gq0")
            if KP0 >= 4:
                W(gq, "agr", 1)
                INC(gq.collective_compute(
                    "AllGather", ALU.bypass, replica_groups=[list(range(NCORES))],
                    ins=[wiT_in[:]], outs=[wiT_ag[:]]), "ccs")
                W(gq, "agr", 2)
                INC(gq.collective_compute(
                    "AllGather", ALU.bypass, replica_groups=[list(range(NCORES))],
                    ins=[woT_in[:]], outs=[woT_ag[:]]), "ccs")
                W(gq, "agr", 3)
                INC(gq.collective_compute(
                    "AllGather", ALU.bypass, replica_groups=[list(range(NCORES))],
                    ins=[wlT_in[:]], outs=[wlT_ag[:]]), "ccs")
            bar(gq, "gpsimd")

        @block.vector
        def _(v):
            W(v, "ld0", 11 * 16)
            v.tensor_copy(out=id_b[:], in_=id_f[:])
            v.tensor_copy(out=tri_b[:], in_=tri_f[:])
            v.tensor_copy(out=gne_b[:], in_=gne_f[:])
            INC(v.tensor_tensor(out=lutd[:], in0=lut4_t[:, :, 1::2],
                                in1=lut4_t[:, :, 0::2], op=ALU.subtract), "vc")
            for j in range(NCAST if KP0 >= 2 else 0):
                W(v, "cld0", 16 * (j + 1))
                if j >= 1:
                    W(v, "wb0", 16 * j)
                INC(v.tensor_copy(out=c_b[:], in_=c_f[:]), "cdone")
            for ot in range(NOT if KP0 >= 3 else 0):
                if KP3 < 2:
                    break
                W(v, "bld0", 16 * (ot + 1))
                for st in range(NSTRIP):
                    nst = ot * NSTRIP + st
                    sl = slice(st * ISTRIP, (st + 1) * ISTRIP)
                    v.tensor_copy(out=bitb[:, 0], in_=bitu[:, 0, sl])
                    v.drain()
                    for c in range(8):
                        v.tensor_scalar(out=arm[c][:], in0=bitb[:, 0],
                                        scalar1=lutd[:, ot, c:c + 1],
                                        scalar2=lut4_t[:, ot, 2 * c:2 * c + 1],
                                        op0=ALU.mult, op1=ALU.add)
                    v.drain()
                    for c in range(4):
                        v.copy_predicated(out=arm[2 * c][:],
                                          mask=bitu[:, 1, sl],
                                          data=arm[2 * c + 1][:])
                    v.drain()
                    v.copy_predicated(out=arm[0][:], mask=bitu[:, 2, sl],
                                      data=arm[2][:])
                    v.copy_predicated(out=arm[4][:], mask=bitu[:, 2, sl],
                                      data=arm[6][:])
                    v.drain()
                    INC(v.copy_predicated(out=arm[0][:], mask=bitu[:, 3, sl],
                                          data=arm[4][:]), "bitc")
                    for it in range(NITS if KP3 >= 3 else 0):
                        W(v, "tp", nst * NITS + it + 1)
                        itg = st * NITS + it
                        INC(v.tensor_copy(
                            out=wlT_sb[:, itg, :],
                            in_=p_tb[(nst * NITS + it) % 2][:]), "ev")
            bar(v, "vector")

        @block.scalar
        def _(sc):
            for j in range(NCAST if KP0 >= 2 else 0):
                W(sc, "cdone", j + 1)
                dst = (wiT_in[j] if j < CTS
                       else woT_in[(j - CTS) * 128:(j - CTS + 1) * 128])
                INC(sc.dma_start(out=dst, in_=c_b[:]), "wb0", 16)
                if j == CTS - 1:
                    W(sc, "wb0", 16 * CTS)
                    sc.sem_inc(S["agr"].h, 1)
                if j == NCAST - 1:
                    W(sc, "wb0", 16 * NCAST)
                    sc.sem_inc(S["agr"].h, 1)
            for ot in range(NOT if (KP0 >= 3 and KP3 >= 4) else 0):
                W(sc, "ev", (ot + 1) * NIT)
                INC(sc.dma_start(
                    out=wlT_in[:, :, ot * 128:(ot + 1) * 128]
                    .transpose([1, 0, 2]), in_=wlT_sb[:]), "wbl", 16)
            if KP0 >= 3 and KP3 >= 4:
                W(sc, "wbl", 16 * NOT)
                sc.sem_inc(S["agr"].h, 1)
            bar(sc, "scalar")

        @block.tensor
        def _(te):
            W(te, "vc", 1)
            ntp = 0
            for ot in range(NOT if (KP0 >= 3 and KP3 >= 3) else 0):
                for st in range(NSTRIP):
                    W(te, "bitc", ot * NSTRIP + st + 1)
                    if ot >= 1 and st == 0 and KP3 >= 4:
                        W(te, "wbl", 16 * ot)
                    for it in range(NITS):
                        if ntp >= 2:
                            W(te, "ev", ntp - 1)
                        INC(te.transpose(out=p_tb[ntp % 2][:],
                                         in_=arm[0][:, it * 128:(it + 1) * 128],
                                         identity=id_b[:]), "tp")
                        ntp += 1
            bar(te, "tensor")

    if PH <= 0:
        es.close()
        return nc
    with contextlib.ExitStack() as px:
        sbl = lambda n, s, d: px.enter_context(nc.sbuf_tensor(n, s, d))
        xt = [sbl(f"xt{i}", [128, DM], F32) for i in range(2)]
        xn_b = [sbl(f"xn_b{i}", [128, DM], BF16) for i in range(2)]
        wsl = [sbl(f"wsl{i}", [128, NDT * 128], BF16) for i in range(2)]
        hstg = [sbl("hstg0", [128, TOKH + 3], F32)] * 2
        cacc = sbl("cacc", [128, TOKH], F32)
        cstg = [sbl("cstg0", [128, TOKH], BF16)] * 2
        gstg = [sbl("gstg0", [128, TOKH], BF16)] * 2
        halo_sv = sbl("halo_sv", [128, NXT, 3], F32)
        NCHH = TOKH // QCH

        for hh in range(2):
            NTA = NTTH + (1 if hh == 0 else 0)
            B = {k: S[k].n for k in S}
            entry_all()

            @block.sync
            def _(sq, hh=hh, NTA=NTA, B=B):
                for g in range(NTA):
                    if g >= 2:
                        W(sq, "lna", B["lna"] + g - 1)
                    src = (x_in[(hh * NTTH + g) * 128:(hh * NTTH + g + 1) * 128]
                           if g < NTTH else xhalo)
                    INC(sq.dma_start(out=xt[g % 2][:], in_=src),
                        "xld0" if g % 2 == 0 else "xld1", 16)
                bar(sq, "sync")

            @block.vector
            def _(v, hh=hh, NTA=NTA, B=B):
                nev = 0
                for g in range(NTA):
                    W(v, "xld0" if g % 2 == 0 else "xld1",
                      B["xld0" if g % 2 == 0 else "xld1"] + 16 * (g // 2 + 1))
                    v.tensor_reduce(out=st1[:, 0:1], in_=xt[g % 2][:], axis=AX.X,
                                    op=ALU.add)
                    INC(v.drain(), "lnv")
                    W(v, "lns", B["lns"] + 2 * g + 1)
                    v.tensor_scalar(out=st1[:, 2:3], in0=st1[:, 0:1],
                                    scalar1=1.0 / DM, scalar2=None, op0=ALU.mult)
                    v.drain()
                    v.tensor_tensor(out=st1[:, 3:4], in0=st1[:, 2:3],
                                    in1=st1[:, 2:3], op=ALU.mult)
                    v.tensor_scalar(out=st1[:, 4:5], in0=st1[:, 1:2],
                                    scalar1=1.0 / DM, scalar2=None, op0=ALU.mult)
                    v.drain()
                    v.tensor_tensor(out=st1[:, 5:6], in0=st1[:, 4:5],
                                    in1=st1[:, 3:4], op=ALU.subtract)
                    v.drain()
                    v.tensor_scalar(out=st1[:, 6:7], in0=st1[:, 5:6],
                                    scalar1=EPS, scalar2=None, op0=ALU.add)
                    INC(v.drain(), "lnv")
                    W(v, "lns", B["lns"] + 2 * g + 2)
                    v.reciprocal(out=st1[:, 8:9], in_=st1[:, 7:8])
                    v.drain()
                    v.tensor_scalar(out=st1[:, 9:10], in0=st1[:, 2:3],
                                    scalar1=st1[:, 8:9], scalar2=-1.0,
                                    op0=ALU.mult, op1=ALU.mult)
                    INC(v.drain(), "lnv")
                    ncol = 3 if g == NTTH else 128
                    for dt in range(NDT):
                        W(v, "tp", B["tp"] + g * NDT + dt + 1)
                        dst_c = (slice(TOKH, TOKH + 3) if g == NTTH
                                 else slice(g * 128, (g + 1) * 128))
                        INC(v.tensor_copy(
                            out=xnfm[:, dt, dst_c],
                            in_=p_tb[(g * NDT + dt) % 2][:, 0:ncol]), "ev")
                bar(v, "vector")

            @block.scalar
            def _(sc, hh=hh, NTA=NTA, B=B):
                for g in range(NTA):
                    W(sc, "lnv", B["lnv"] + 3 * g + 1)
                    INC(sc.activation(out=wsl[0][:], in_=xt[g % 2][:],
                                      func=AF.Square, accum_out=st1[:, 1:2]),
                        "lns")
                    W(sc, "lnv", B["lnv"] + 3 * g + 2)
                    INC(sc.activation(out=st1[:, 7:8], in_=st1[:, 6:7],
                                      func=AF.Sqrt), "lns")
                    W(sc, "lnv", B["lnv"] + 3 * g + 3)
                    if g >= 2:
                        W(sc, "tp", B["tp"] + (g - 1) * NDT)
                    INC(sc.activation(out=xn_b[g % 2][:], in_=xt[g % 2][:],
                                      func=AF.Identity, scale=st1[:, 8:9],
                                      bias=st1[:, 9:10]), "lna")
                bar(sc, "scalar")

            @block.tensor
            def _(te, hh=hh, NTA=NTA, B=B):
                for g in range(NTA):
                    W(te, "lna", B["lna"] + g + 1)
                    for dt in range(NDT):
                        n = g * NDT + dt
                        if n >= 2:
                            W(te, "ev", B["ev"] + n - 1)
                        INC(te.transpose(
                            out=p_tb[n % 2][:],
                            in_=xn_b[g % 2][:, dt * 128:(dt + 1) * 128],
                            identity=id_b[:]), "tp")
                bar(te, "tensor")

            @block.gpsimd
            def _(gq):
                bar(gq, "gpsimd")

            B = {k: S[k].n for k in S}
            entry_all()

            @block.sync
            def _(sq, hh=hh, B=B):
                for ct in range(NCT):
                    if ct >= 2:
                        W(sq, "mmr", B["mmr"] + ct - 1)
                    INC(sq.dma_start(out=wsl[ct % 2][:], in_=wiT_ag[ct]),
                        "wld0" if ct % 2 == 0 else "wld1", 16)
                bar(sq, "sync")

            @block.tensor
            def _(te, hh=hh, B=B):
                if hh == 0:
                    W(te, "ccs", 1)
                for ct in range(NCT):
                    W(te, "wld0" if ct % 2 == 0 else "wld1",
                      B["wld0" if ct % 2 == 0 else "wld1"] + 16 * (ct // 2 + 1))
                    if ct >= 1:
                        W(te, "evc", B["evc"] + ct)
                    for dt in range(NDT):
                        lhsT = wsl[ct % 2][:, dt * 128:(dt + 1) * 128]
                        st, sp = dt == 0, dt == NDT - 1
                        for k in range(NTC):
                            mm = te.matmul(
                                out=pA[k][:, 0:TCH], lhsT=lhsT,
                                rhs=xnfm[:, dt, k * TCH:(k + 1) * TCH],
                                start=st, stop=sp)
                        if hh == 0:
                            mm = te.matmul(out=p_h3[:, 0:3], lhsT=lhsT,
                                           rhs=xnfm[:, dt, TOKH:TOKH + 3],
                                           start=st, stop=sp)
                    INC(mm, "mmr")
                bar(te, "tensor")

            @block.vector
            def _(v, hh=hh, B=B):
                for ct in range(NCT):
                    if ct < 32:
                        continue
                    W(v, "mmr", B["mmr"] + ct + 1)
                    if ct < 80:
                        xi = ct - 32
                        v.drain()
                        for k in range(NTC):
                            v.tensor_copy(
                                out=hstg[xi % 2][:, 3 + k * TCH:3 + (k + 1) * TCH],
                                in_=pA[k][:, 0:TCH])
                        if hh == 0:
                            INC(v.tensor_copy(out=hstg[xi % 2][:, 0:3],
                                              in_=p_h3[:, 0:3]), "evc")
                        else:
                            INC(v.tensor_copy(out=hstg[xi % 2][:, 0:3],
                                              in_=halo_sv[:, xi, :]), "evc")
                        v.drain()
                        if hh == 0:
                            v.tensor_copy(out=halo_sv[:, xi, :],
                                          in_=hstg[xi % 2][:, TOKH:TOKH + 3])
                        if xi >= 1:
                            W(v, "scx", B["scx"] + xi)
                        v.drain()
                        v.tensor_scalar(out=cacc[:], in0=hstg[xi % 2][:, 0:TOKH],
                                        scalar1=convw_s[:, xi, 0:1], scalar2=None,
                                        op0=ALU.mult)
                        for k in range(1, KC):
                            v.drain()
                            v.scalar_tensor_tensor(
                                out=cacc[:], in0=hstg[xi % 2][:, k:TOKH + k],
                                scalar=convw_s[:, xi, k:k + 1], in1=cacc[:],
                                op0=ALU.mult, op1=ALU.add)
                        INC(v.drain(), "cvv")
                    else:
                        v.drain()
                        for k in range(NTC):
                            mm = v.tensor_copy(
                                out=dt_fm[0:64,
                                          hh * TOKH + k * TCH:
                                          hh * TOKH + (k + 1) * TCH],
                                in_=pA[k][0:64, 0:TCH])
                        INC(mm, "evc")
                        INC(v.drain(), "dtq")
                bar(v, "vector")

            @block.scalar
            def _(sc, hh=hh, B=B):
                for ct in range(NCT):
                    if ct < 32:
                        W(sc, "mmr", B["mmr"] + ct + 1)
                        if ct >= 1:
                            W(sc, "spg" if (ct - 1) % 2 == 0 else "spg1",
                              B["spg" if (ct - 1) % 2 == 0 else "spg1"]
                              + 16 * ((ct - 1) // 2 + 1))
                        for k in range(NTC):
                            mm = sc.activation(
                                out=gstg[ct % 2][:, k * TCH:(k + 1) * TCH],
                                in_=pA[k][:, 0:TCH], func=AF.Silu)
                        INC(mm, "evc")
                        INC(sc.drain(), "scg")
                    elif ct < 80:
                        xi = ct - 32
                        W(sc, "cvv", B["cvv"] + xi + 1)
                        if xi >= 1:
                            W(sc, "spx" if (xi - 1) % 2 == 0 else "spx1",
                              B["spx" if (xi - 1) % 2 == 0 else "spx1"]
                              + 16 * ((xi - 1) // 2 + 1))
                        INC(sc.activation(out=cstg[xi % 2][:], in_=cacc[:],
                                          func=AF.Silu,
                                          bias=convb_s[:, xi:xi + 1]), "scx")
                bar(sc, "scalar")

            @block.gpsimd
            def _(gq, hh=hh, B=B):
                for gi in range(32):
                    W(gq, "scg", B["scg"] + gi + 1)
                    INC(gq.dma_start(
                        out=gate_sp[hh * NCHH:(hh + 1) * NCHH, gi]
                        .transpose([1, 0, 2]),
                        in_=gstg[gi % 2][:].rearrange("p (c q) -> p c q",
                                                      c=NCHH)),
                        "spg" if gi % 2 == 0 else "spg1", 16)
                for xi in range(48):
                    W(gq, "scx", B["scx"] + xi + 1)
                    INC(gq.dma_start(
                        out=xbc_sp[hh * NCHH:(hh + 1) * NCHH, xi]
                        .transpose([1, 0, 2]),
                        in_=cstg[xi % 2][:].rearrange("p (c q) -> p c q",
                                                      c=NCHH)),
                        "spx" if xi % 2 == 0 else "spx1", 16)
                bar(gq, "gpsimd")

    if PH <= 1:
        es.close()
        return nc
    with contextlib.ExitStack() as px:
        sbl = lambda n, s, d: px.enter_context(nc.sbuf_tensor(n, s, d))
        spa = sbl("spa", [64, TOK], F32)
        spb = sbl("spb", [64, TOK], F32)

        B = {k: S[k].n for k in S}
        entry_all()

        @block.vector
        def _(v, B=B):
            W(v, "dtq", B["dtq"])
            v.tensor_scalar(out=spb[:], in0=dt_fm[:], scalar1=dtb_s[:],
                            scalar2=None, op0=ALU.add)
            v.drain()
            v.tensor_scalar(out=spa[:], in0=spb[:], scalar1=-1.0,
                            scalar2=None, op0=ALU.mult)
            v.drain()
            INC(v.tensor_tensor(out=spa[:], in0=spa[:], in1=spb[:],
                                op=ALU.max), "dts")
            W(v, "dsv", B["dsv"] + 1)
            v.tensor_tensor(out=dt_fm[:], in0=spb[:], in1=spa[:], op=ALU.add)
            v.drain()
            INC(v.tensor_scalar(out=nega_s[:], in0=nega_s[:], scalar1=-1.0,
                                scalar2=None, op0=ALU.mult), "dts")
            v.drain()
            v.tensor_scalar(out=spa[:], in0=dt_fm[:], scalar1=nega_s[:],
                            scalar2=None, op0=ALU.mult)
            v.drain()
            for c in range(NCH):
                v.tensor_tensor_scan(
                    out=lloc[:, c * QCH:(c + 1) * QCH],
                    data0=spa[:, c * QCH:(c + 1) * QCH],
                    data1=zer128[0:64, 0:QCH], initial=0.0,
                    op0=ALU.add, op1=ALU.add)
            INC(v.drain(), "dts")
            W(v, "dsv", B["dsv"] + 2)
            for c in range(NCH):
                v.tensor_tensor(
                    out=spb[:, c * QCH:(c + 1) * QCH],
                    in0=lloc[:, c * QCH + QCH - 1:c * QCH + QCH]
                    .broadcast_to((64, QCH)),
                    in1=lloc[:, c * QCH:(c + 1) * QCH], op=ALU.subtract)
            INC(v.drain(), "dts")
            v.tensor_copy(out=dect[:], in_=cum_a[:, QCH - 1::QCH])
            INC(v.drain(), "dts")
            for k in range(-(-HCH // 512)):
                wdt = min(512, HCH - k * 512)
                W(v, "pex", B["pex"] + k + 1)
                v.tensor_copy(out=dkbc[:, k * 512:k * 512 + wdt],
                              in_=pA[k % 2][:, 0:wdt])
            INC(v.drain(), "dts")
            bar(v, "vector")

        @block.scalar
        def _(sc, B=B):
            W(sc, "dts", B["dts"] + 1)
            sc.activation(out=spa[:], in_=spa[:], func=AF.Exp, scale=-1.0)
            sc.drain()
            sc.activation(out=spa[:], in_=spa[:], func=AF.Ln, bias=1.0)
            sc.activation(out=spb[:], in_=dt_fm[:], func=AF.Relu, bias=dtb_s[:])
            INC(sc.activation(out=nega_s[:], in_=alog_s[:], func=AF.Exp), "dsv")
            W(sc, "dts", B["dts"] + 3)
            INC(sc.activation(out=cum_a[:], in_=lloc[:], func=AF.Exp), "dsv")
            W(sc, "dts", B["dts"] + 4)
            INC(sc.activation(out=rev_a[:], in_=spb[:], func=AF.Exp), "dsv")
            bar(sc, "scalar")

        @block.sync
        def _(sq, B=B):
            W(sq, "dts", B["dts"] + 3)
            INC(sq.dma_start(out=lfl_d, in_=lloc[:]), "lf0", 16)
            W(sq, "dsv", B["dsv"] + 2)
            INC(sq.dma_start(out=cfl_d, in_=cum_a[:]), "lf0", 16)
            W(sq, "dts", B["dts"] + 5)
            INC(sq.dma_start(out=dkfl[:], in_=dect[:]), "lf0", 16)
            bar(sq, "sync")

        @block.tensor
        def _(te, B=B):
            W(te, "lf0", B["lf0"] + 48)
            for k in range(-(-HCH // 512)):
                wdt = min(512, HCH - k * 512)
                INC(te.matmul(out=pA[k % 2][:, 0:wdt],
                              lhsT=ones1f[0:1, 0:128],
                              rhs=dkfl[0:1, k * 512:k * 512 + wdt],
                              start=True, stop=True), "pex")
            bar(te, "tensor")

        @block.gpsimd
        def _(gq):
            bar(gq, "gpsimd")

    if PH <= 2:
        es.close()
        return nc
    with contextlib.ExitStack() as px:
        sbl = lambda n, s, d: px.enter_context(nc.sbuf_tensor(n, s, d))
        xs_b = sbl("xs_b", [128, NIT, QCH], BF16)
        bc_bt = sbl("bc_bt", [128, 16, QCH], BF16)
        gat_t = sbl("gat_t", [128, NIT, QCH], BF16)
        u_t = sbl("u_t", [128, NIT, QCH], BF16)
        ue_t = sbl("ue_t", [128, NIT, QCH], BF16)
        btm = sbl("btm", [128, G, N], BF16)
        dtT_c = sbl("dtT_c", [128, 64], F32)
        revT_c = sbl("revT_c", [128, 64], F32)
        llT_c = sbl("llT_c", [128, 64], F32)
        scT_c = sbl("scT_c", [128, 64], F32)
        gt_g = sbl("gt_g", [128, 128], BF16)
        dstg = sbl("dstg", [128, HPG * QCH], F32)
        expd = sbl("expd", [128, HPG * QCH], BF16)
        cp_g = sbl("cp_g", [128, HPG, QCH], BF16)
        y_f = sbl("y_f", [128, NIT, QCH], BF16)
        s_b = sbl("s_b", [128, H * 64], BF16)
        sqstg = [sbl(f"sqstg{i}", [128, 128], BF16) for i in range(2)]
        rs_t = sbl("rs_t", [128, 8], F32)
        rs_sd = sbl("rs_sd", [128, 8], F32)
        rs_ft = sbl("rs_ft", [8, 128], BF16)
        gnt_f = sbl("gnt_f", [8, NIT, 128], BF16)
        lcfl = sbl("lcfl", [65, 8 * QCH], F32)
        lflc = [lcfl[0:1, :], lcfl[32:33, :]]
        cflc = [lcfl[64:65, :], lcfl[64:65, :]]

        B = {k: S[k].n for k in S}
        entry_all()

        @block.sync
        def _(sq, B=B):
            for c in range(NCH):
                if c >= 1:
                    W(sq, "tp2", B["tp2"] + c * 40)
                INC(sq.dma_start(out=xs_b[:],
                                 in_=xbc_sp[c, 0:NIT].transpose([1, 0, 2])),
                    "xrl0", 16)
                INC(sq.dma_start(out=bc_bt[:],
                                 in_=xbc_sp[c, NIT:NXT].transpose([1, 0, 2])),
                    "xrl0", 16)
            INC(sq.dma_start(out=gnt_f[:], in_=gnt_in), "grl", 16)
            W(sq, "st2", B["st2"] + 64 * NCH)
            INC(sq.dma_start(out=st_in, in_=s_f[:]), "cq", 16)
            W(sq, "ccs", 4)
            INC(sq.dma_start(out=s_f[:], in_=st_ag[0]), "cq", 16)
            bar(sq, "sync")

        @block.tensor
        def _(te, B=B):
            for c in range(NCH):
                W(te, "xrl0", B["xrl0"] + 32 * (c + 1))
                if c >= 1:
                    W(te, "vb2", B["vb2"] + 10 * (c - 1) + 2)
                INC(te.transpose(out=p_tf[0][:, 0:64],
                                 in_=dt_fm[0:64, c * QCH:(c + 1) * QCH],
                                 identity=id_f[0:64, 0:64]), "tpf")
                INC(te.transpose(out=p_tf[1][:, 0:64],
                                 in_=rev_a[0:64, c * QCH:(c + 1) * QCH],
                                 identity=id_f[0:64, 0:64]), "tpf")
                for j in range(NIT):
                    m = c * NIT + j
                    if j == 0 and c >= 1:
                        W(te, "vb2", B["vb2"] + 10 * (c - 1) + 10)
                    if m >= 2:
                        W(te, "ue2", B["ue2"] + m - 1)
                    INC(te.transpose(out=p_tb[m % 2][:],
                                     in_=xs_b[:, j, :], identity=id_b[:]),
                        "tp2")
                for g in range(G):
                    m = c * G + g
                    if g == 0:
                        W(te, "ue2", B["ue2"] + 32 * (c + 1))
                    if m >= 2:
                        W(te, "vb2", B["vb2"] + 10 * (m // G) + 2 + (m % G) - 1) \
                            if False else W(te, "vb2",
                                            B["vb2"] + 2 * (m // G + 1) + m - 1)
                    INC(te.transpose(out=p_tc[m % 2][:],
                                     in_=bc_bt[:, g, :], identity=id_b[:]),
                        "tp2")
                for j in range(NIT - G):
                    INC(te.sem_inc(S["tp2"].h, 0), "tp2") if False else None
                S["tp2"].n += 0
                for h in range(H):
                    g, it = h // HPG, h // 2
                    W(te, "ue2", B["ue2"] + c * NIT + it + 1)
                    W(te, "vb2", B["vb2"] + c * 10 + 2 + g + 1)
                    if h >= 2:
                        W(te, "st2", B["st2"] + c * H + h - 1)
                    INC(te.matmul(out=p_x[h % 2][:], lhsT=btm[:, g],
                                  rhs=ue_t[:, it, (h % 2) * 64:(h % 2 + 1) * 64],
                                  start=True, stop=True), "pex")
            bar(te, "tensor")

        @block.vector
        def _(v, B=B):
            for c in range(NCH):
                W(v, "tpf", B["tpf"] + 2 * c + 2)
                v.tensor_copy(out=dtT_c[:], in_=p_tf[0][:, 0:64])
                INC(v.tensor_copy(out=revT_c[:], in_=p_tf[1][:, 0:64]), "vb2")
                v.drain()
                INC(v.tensor_tensor(out=scT_c[:], in0=dtT_c[:], in1=revT_c[:],
                                    op=ALU.mult), "vb2")
                for g in range(G):
                    m = c * G + g
                    W(v, "tp2", B["tp2"] + c * 40 + NIT + g + 1)
                    INC(v.tensor_copy(out=btm[:, g], in_=p_tc[m % 2][:]), "vb2")
                for h in range(H):
                    W(v, "pex", B["pex"] + c * H + h + 1)
                    INC(v.scalar_tensor_tensor(
                        out=s_f[:, h * 64:(h + 1) * 64],
                        in0=s_f[:, h * 64:(h + 1) * 64],
                        scalar=dkbc[:, h * NCH + c:h * NCH + c + 1],
                        in1=p_x[h % 2][:], op0=ALU.mult, op1=ALU.add), "st2")
                v.drain()
            W(v, "cq", B["cq"] + 32)
            INC(v.tensor_scalar(out=s_f[:], in0=s_f[:], scalar1=ssc_s[:],
                                scalar2=None, op0=ALU.mult), "sti")
            v.drain()
            bar(v, "vector")

        @block.scalar
        def _(sc, B=B):
            for c in range(NCH):
                W(sc, "vb2", B["vb2"] + c * 10 + 2)
                for j in range(NIT):
                    W(sc, "tp2", B["tp2"] + c * 40 + j + 1)
                    sc.activation(out=ue_t[:, j, 0:64], in_=p_tb[(c * NIT + j) % 2][:, 0:64],
                                  func=AF.Identity, scale=scT_c[:, 2 * j:2 * j + 1])
                    INC(sc.activation(out=ue_t[:, j, 64:128],
                                      in_=p_tb[(c * NIT + j) % 2][:, 64:128],
                                      func=AF.Identity,
                                      scale=scT_c[:, 2 * j + 1:2 * j + 2]), "ue2")
            bar(sc, "scalar")

        @block.gpsimd
        def _(gq, B=B):
            W(gq, "cq", B["cq"] + 16)
            INC(gq.collective_compute(
                "AllGather", ALU.bypass,
                replica_groups=[[i, i + 1] for i in range(0, NCORES, 2)],
                ins=[st_in[:]], outs=[st_ag[:]]), "ccs")
            bar(gq, "gpsimd")

        if KP2 >= 2:
            for nm in ("gmm", "lbm", "cum", "gnc"):
                S[nm] = Sem(nm)
            B = {k: S[k].n for k in S}
            entry_all()

            @block.sync
            def _(sq, B=B):
                for c in range(NCH):
                    if c >= 1:
                        W(sq, "tp2", B["tp2"] + 40 * c)
                        W(sq, "yev", B["yev"] + 32 * c)
                        W(sq, "cpq", B["cpq"] + 64 * c)
                        W(sq, "ynd", B["ynd"] + 32 * c)
                    INC(sq.dma_start(out=xs_b[:],
                                     in_=xbc_sp[c, 0:NIT].transpose([1, 0, 2])),
                        "xrl1", 16)
                    INC(sq.dma_start(out=bc_bt[:],
                                     in_=xbc_sp[c, NIT:NXT].transpose([1, 0, 2])),
                        "xrl1", 16)
                    INC(sq.dma_start(out=gat_t[:],
                                     in_=gate_sp[c].transpose([1, 0, 2])),
                        "xrl1", 16)
                    for g in range(G):
                        m = c * G + g
                        if m >= 2:
                            W(sq, "lbm", B["lbm"] + 64 * ((m - 2) // G)
                              + 8 * ((m - 2) % G) + 8)
                        if m >= 1:
                            W(sq, "cum", B["cum"] + 64 * ((m - 1) // G)
                              + 8 * ((m - 1) % G) + 8)
                        INC(sq.dma_start(
                            out=lflc[g % 2][:],
                            in_=lfl_d[8 * g:8 * g + 8, c * QCH:(c + 1) * QCH]),
                            "gld0" if g % 2 == 0 else "gld1", 16)
                        INC(sq.dma_start(
                            out=cflc[g % 2][:],
                            in_=cfl_d[8 * g:8 * g + 8, c * QCH:(c + 1) * QCH]),
                            "gld0" if g % 2 == 0 else "gld1", 16)
                bar(sq, "sync")

            @block.tensor
            def _(te, B=B):
                W(te, "sti", B["sti"])
                for c in range(NCH):
                    W(te, "xrl1", B["xrl1"] + 48 * (c + 1))
                    if c >= 1:
                        W(te, "vb2", B["vb2"] + 13 * (c - 1) + 13)
                        W(te, "ynd", B["ynd"] + 32 * c)
                    INC(te.transpose(out=p_tf[0][:, 0:64],
                                     in_=dt_fm[0:64, c * QCH:(c + 1) * QCH],
                                     identity=id_f[0:64, 0:64]), "tpf")
                    W(te, "vb2", B["vb2"] + 13 * c + 1)
                    INC(te.transpose(out=p_tf[1][:, 0:64],
                                     in_=rev_a[0:64, c * QCH:(c + 1) * QCH],
                                     identity=id_f[0:64, 0:64]), "tpf")
                    W(te, "vb2", B["vb2"] + 13 * c + 2)
                    INC(te.transpose(out=p_tf[0][:, 0:64],
                                     in_=lloc[0:64, c * QCH:(c + 1) * QCH],
                                     identity=id_f[0:64, 0:64]), "tpf")
                    for j in range(NIT):
                        m = c * NIT + j
                        if j == 0 and c >= 1:
                            W(te, "vb2", B["vb2"] + 13 * (c - 1) + 12)
                        if m >= 2:
                            W(te, "ue2", B["ue2"] + m - 1)
                            W(te, "ut2", B["ut2"] + m - 1)
                        INC(te.transpose(out=p_tb[m % 2][:], in_=xs_b[:, j, :],
                                         identity=id_b[:]), "tp2")
                    for g in range(G):
                        m = c * G + g
                        if g == 0:
                            W(te, "ue2", B["ue2"] + 32 * (c + 1))
                            W(te, "ut2", B["ut2"] + 32 * (c + 1))
                        if m >= 2:
                            cc2, gg2 = divmod(m - 2, G)
                            W(te, "vb2", B["vb2"] + 13 * cc2 + 5 + gg2)
                        INC(te.transpose(out=p_tc[m % 2][:], in_=bc_bt[:, g, :],
                                         identity=id_b[:]), "tp2")
                    for g in range(G):
                        W(te, "gev", B["gev"] + 8 * c + g)
                        W(te, "yev", B["yev"] + 32 * c + 4 * g)
                        INC(te.matmul(out=p_g[:], lhsT=bc_bt[:, g, :],
                                      rhs=bc_bt[:, 8 + g, :],
                                      start=True, stop=True), "gmm")
                        m = c * G + g
                        W(te, "gld0" if g % 2 == 0 else "gld1",
                          B["gld0" if g % 2 == 0 else "gld1"] + 32 * (m // 2 + 1))
                        for j in range(G):
                            l = 64 * c + 8 * g + j
                            if j == 0:
                                W(te, "cpq", B["cpq"] + 64 * c + 8 * g)
                                W(te, "st2", B["st2"] + 64 * c + 8 * g)
                            if l >= 2:
                                W(te, "dsv", B["dsv"] + l - 1)
                            lbase = 0 if g % 2 == 0 else 32
                            INC(te.matmul(
                                out=p_lb[j % 2][:],
                                lhsT=ones65[lbase:lbase + 1, :],
                                rhs=lflc[g % 2][0:1, j * QCH:(j + 1) * QCH],
                                start=True, stop=True), "lbm")
                        for j in range(G):
                            l = 64 * c + 8 * g + j
                            if j == 0:
                                W(te, "dsv", B["dsv"] + 64 * c + 8 * g + 8)
                            if l >= 2:
                                W(te, "cpq", B["cpq"] + l - 1)
                            INC(te.matmul(
                                out=p_cu[j % 2][:],
                                lhsT=ones65[64:65, :],
                                rhs=cflc[g % 2][0:1, j * QCH:(j + 1) * QCH],
                                start=True, stop=True), "cum")
                        W(te, "mtq", B["mtq"] + 8 * c + g + 1)
                        W(te, "sbc", B["sbc"] + c + 1)
                        for j2 in range(4):
                            it = 4 * g + j2
                            W(te, "cpq", B["cpq"] + 64 * c + 8 * g + 2 * j2 + 2)
                            W(te, "ut2", B["ut2"] + 32 * c + it + 1)
                            if 32 * c + it >= 1:
                                W(te, "yev", B["yev"] + 32 * c + it - 1)
                            for hh2 in range(2):
                                h = 8 * g + 2 * j2 + hh2
                                r0, r1 = hh2 * 64, hh2 * 64 + 64
                                if "2" in KNEU:
                                    r0, r1 = 0, 64
                                te.matmul(out=p_y[it % 2][r0:r1, :],
                                          lhsT=u_t[:, it, r0:r1],
                                          rhs=expd[:, (2 * j2 + hh2) * QCH:
                                                   (2 * j2 + hh2 + 1) * QCH],
                                          start=True, stop=False)
                                mm = te.matmul(out=p_y[it % 2][r0:r1, :],
                                               lhsT=s_b[:, h * 64:(h + 1) * 64],
                                               rhs=cp_g[:, 2 * j2 + hh2, :],
                                               start=False, stop=True)
                            INC(mm, "ymd")
                        for j in range(G):
                            h = 8 * g + j
                            it = h // 2
                            W(te, "ue2", B["ue2"] + 32 * c + it + 1)
                            l = 64 * c + 8 * g + j
                            if l >= 2:
                                W(te, "st2", B["st2"] + l - 1)
                            INC(te.matmul(
                                out=p_x[j % 2][:], lhsT=btm[:, g],
                                rhs=ue_t[:, it, (h % 2) * 64:(h % 2) * 64 + 64],
                                start=True, stop=True), "pex")
                    for it in range(NIT):
                        if it == 0:
                            W(te, "yev", B["yev"] + 32 * (c + 1))
                        W(te, "sqq", B["sqq"] + 32 * c + it + 1)
                        mm = te.matmul(out=p_gn[:], lhsT=sqstg[it % 2][:],
                                       rhs=gne_b[:, it],
                                       start=(it == 0) or "1" in KNEU,
                                       stop=(it == NIT - 1) or "1" in KNEU)
                        INC(mm, "gnc")
                    W(te, "rsq", B["rsq"] + 3 * c + 3)
                    W(te, "vb2", B["vb2"] + 13 * c + 3)
                    INC(te.transpose(out=p_tf[0][0:8, 0:128], in_=rs_t[:],
                                     identity=id_f[:]), "tpf")
                    W(te, "vb2", B["vb2"] + 13 * c + 13)
                    if c == 0:
                        W(te, "grl", 16)
                    for it in range(NIT):
                        if 32 * c + it >= 2:
                            W(te, "ynd", B["ynd"] + 32 * c + it - 1)
                        INC(te.matmul(out=p_rs[it % 2][:],
                                      lhsT=(rs_ft[:] if "3" in KNEU
                                            else gnt_f[:, it]),
                                      rhs=rs_ft[:], start=True, stop=True), "rsm")
                bar(te, "tensor")

            @block.vector
            def _(v, B=B):
                for c in range(NCH):
                    if c >= 1:
                        v.drain()
                    W(v, "sqq", B["sqq"] + 32 * c)
                    W(v, "spy", B["spy"] + 16 * c)
                    INC(v.tensor_copy(out=s_b[:], in_=s_f[:]), "sbc")
                    W(v, "tpf", B["tpf"] + 4 * c + 1)
                    INC(v.tensor_copy(out=dtT_c[:], in_=(
                        scT_c[:] if "a" in KNR else p_tf[0][:, 0:64])), "vb2")
                    W(v, "tpf", B["tpf"] + 4 * c + 2)
                    INC(v.tensor_copy(out=revT_c[:], in_=(
                        scT_c[:] if "a" in KNR else p_tf[1][:, 0:64])), "vb2")
                    W(v, "tpf", B["tpf"] + 4 * c + 3)
                    INC(v.tensor_copy(out=llT_c[:], in_=(
                        scT_c[:] if "a" in KNR else p_tf[0][:, 0:64])), "vb2")
                    v.drain()
                    INC(v.tensor_tensor(out=scT_c[:], in0=dtT_c[:], in1=revT_c[:],
                                        op=ALU.mult), "vb2")
                    for j in range(NIT):
                        W(v, "tp2", B["tp2"] + 40 * c + j + 1)
                        ptbj = (xs_b[:, j, :] if "b" in KNR
                                else p_tb[(c * NIT + j) % 2][:])
                        v.tensor_scalar(out=u_t[:, j, 0:64],
                                        in0=ptbj[:, 0:64],
                                        scalar1=dtT_c[:, 2 * j:2 * j + 1],
                                        scalar2=None, op0=ALU.mult)
                        INC(v.tensor_scalar(out=u_t[:, j, 64:128],
                                            in0=ptbj[:, 64:128],
                                            scalar1=dtT_c[:, 2 * j + 1:2 * j + 2],
                                            scalar2=None, op0=ALU.mult), "ut2")
                    for g in range(G):
                        W(v, "tp2", B["tp2"] + 40 * c + NIT + g + 1)
                        INC(v.tensor_copy(out=btm[:, g], in_=(
                            bc_bt[:, g, :] if "c" in KNR
                            else p_tc[(c * G + g) % 2][:])), "vb2")
                    for g in range(G):
                        W(v, "gmm", B["gmm"] + 8 * c + g + 1)
                        INC(v.tensor_tensor(out=gt_g[:], in0=(
                            dstg[:, 0:128] if "d" in KNR else p_g[:]),
                            in1=tri_b[:], op=ALU.mult), "gev")
                        for j in range(G):
                            l = 64 * c + 8 * g + j
                            W(v, "lbm", B["lbm"] + l + 1)
                            INC(v.scalar_tensor_tensor(
                                out=dstg[:, j * QCH:(j + 1) * QCH],
                                in0=(zer128[:] if "e" in KNR
                                     else p_lb[j % 2][:]),
                                scalar=llT_c[:, 8 * g + j:8 * g + j + 1],
                                in1=zer128[:], op0=ALU.subtract, op1=ALU.min),
                                "dsv")
                        W(v, "exq", B["exq"] + 8 * c + g + 1)
                        INC(v.tensor_tensor(
                            out=expd[:], in0=expd[:],
                            in1=gt_g[:].unsqueeze(1).broadcast_to((128, G, QCH)),
                            op=ALU.mult), "mtq")
                        if 8 * c + g >= 1:
                            W(v, "ymd", B["ymd"] + 32 * c + 4 * g)
                        for j in range(G):
                            l = 64 * c + 8 * g + j
                            W(v, "cum", B["cum"] + l + 1)
                            INC(v.tensor_tensor(out=cp_g[:, j, :],
                                                in0=bc_bt[:, 8 + g, :],
                                                in1=(zer128[:] if "f" in KNR
                                                     else p_cu[j % 2][:]),
                                                op=ALU.mult), "cpq")
                        for j2 in range(4):
                            it = 4 * g + j2
                            W(v, "ymd", B["ymd"] + 32 * c + it + 1)
                            INC(v.scalar_tensor_tensor(
                                out=y_f[:, it, :], in0=xs_b[:, it, :],
                                scalar=dtl_s[:, it:it + 1],
                                in1=(zer128[:] if "g" in KNR
                                     else p_y[it % 2][:]),
                                op0=ALU.mult, op1=ALU.add), "yev")
                        for j in range(G):
                            h = 8 * g + j
                            l = 64 * c + 8 * g + j
                            W(v, "pex", B["pex"] + l + 1)
                            INC(v.scalar_tensor_tensor(
                                out=s_f[:, h * 64:(h + 1) * 64],
                                in0=s_f[:, h * 64:(h + 1) * 64],
                                scalar=dkbc[:, h * NCH + c:h * NCH + c + 1],
                                in1=(zer128[:, 0:64] if "h" in KNR
                                     else p_x[j % 2][:]),
                                op0=ALU.mult, op1=ALU.add),
                                "st2")
                    W(v, "gnc", B["gnc"] + 32 * (c + 1))
                    INC(v.tensor_scalar(out=rs_t[:], in0=(
                        rs_sd[:] if "i" in KNR else p_gn[:]),
                                        scalar1=1.0 / 512, scalar2=EPS,
                                        op0=ALU.mult, op1=ALU.add), "rsq")
                    W(v, "rsq", B["rsq"] + 3 * c + 2)
                    v.reciprocal(out=rs_t[:], in_=rs_sd[:])
                    INC(v.drain(), "rsq")
                    W(v, "tpf", B["tpf"] + 4 * c + 4)
                    INC(v.tensor_copy(out=rs_ft[:], in_=(
                        gnt_f[:, 0] if "a" in KNR
                        else p_tf[0][0:8, 0:128])), "vb2")
                    for it in range(NIT):
                        W(v, "rsm", B["rsm"] + 32 * c + it + 1)
                        v.tensor_tensor(out=y_f[:, it, :], in0=y_f[:, it, :],
                                        in1=gat_t[:, it, :], op=ALU.mult)
                        v.drain()
                        INC(v.scalar_tensor_tensor(
                            out=y_f[:, it, :], in0=y_f[:, it, :],
                            scalar=nw_s[:, it:it + 1],
                            in1=(zer128[:] if "j" in KNR
                                 else p_rs[it % 2][:]),
                            op0=ALU.mult, op1=ALU.mult), "ynd")
                bar(v, "vector")

            @block.scalar
            def _(sc, B=B):
                for c in range(NCH):
                    W(sc, "vb2", B["vb2"] + 13 * c + 4)
                    for j in range(NIT):
                        W(sc, "ut2", B["ut2"] + 32 * c + j + 1)
                        sc.activation(out=ue_t[:, j, 0:64],
                                      in_=u_t[:, j, 0:64],
                                      func=AF.Identity,
                                      scale=revT_c[:, 2 * j:2 * j + 1])
                        INC(sc.activation(out=ue_t[:, j, 64:128],
                                          in_=u_t[:, j, 64:128],
                                          func=AF.Identity,
                                          scale=revT_c[:, 2 * j + 1:2 * j + 2]),
                            "ue2")
                    for g in range(G):
                        W(sc, "dsv", B["dsv"] + 64 * c + 8 * g + 8)
                        if 8 * c + g >= 1:
                            W(sc, "ymd", B["ymd"] + 32 * c + 4 * g)
                        INC(sc.activation(out=expd[:], in_=dstg[:], func=AF.Exp),
                            "exq")
                    for it in range(NIT):
                        W(sc, "yev", B["yev"] + 32 * c + it + 1)
                        m = 32 * c + it
                        if m >= 2:
                            W(sc, "gnc", B["gnc"] + m - 1)
                        INC(sc.activation(out=sqstg[it % 2][:], in_=y_f[:, it, :],
                                          func=AF.Square), "sqq")
                    W(sc, "rsq", B["rsq"] + 3 * c + 1)
                    INC(sc.activation(out=rs_sd[:], in_=rs_t[:], func=AF.Sqrt),
                        "rsq")
                bar(sc, "scalar")

            @block.gpsimd
            def _(gq, B=B):
                for c in range(NCH):
                    W(gq, "ynd", B["ynd"] + 32 * (c + 1))
                    INC(gq.dma_start(out=yn_sp[c].transpose([1, 0, 2]),
                                     in_=y_f[:]), "spy", 16)
                bar(gq, "gpsimd")

    def ln_v(v, n, src, Blnv, Blns):
        v.tensor_reduce(out=st1[:, 0:1], in_=src, axis=AX.X, op=ALU.add)
        INC(v.drain(), "lnv")
        W(v, "lns", Blns + 2 * n + 1)
        v.tensor_scalar(out=st1[:, 2:3], in0=st1[:, 0:1], scalar1=1.0 / DM,
                        scalar2=None, op0=ALU.mult)
        v.drain()
        v.tensor_tensor(out=st1[:, 3:4], in0=st1[:, 2:3], in1=st1[:, 2:3],
                        op=ALU.mult)
        v.tensor_scalar(out=st1[:, 4:5], in0=st1[:, 1:2], scalar1=1.0 / DM,
                        scalar2=None, op0=ALU.mult)
        v.drain()
        v.tensor_tensor(out=st1[:, 5:6], in0=st1[:, 4:5], in1=st1[:, 3:4],
                        op=ALU.subtract)
        v.drain()
        v.tensor_scalar(out=st1[:, 6:7], in0=st1[:, 5:6], scalar1=EPS,
                        scalar2=None, op0=ALU.add)
        INC(v.drain(), "lnv")
        W(v, "lns", Blns + 2 * n + 2)
        v.reciprocal(out=st1[:, 8:9], in_=st1[:, 7:8])
        v.drain()
        v.tensor_scalar(out=st1[:, 9:10], in0=st1[:, 2:3], scalar1=st1[:, 8:9],
                        scalar2=-1.0, op0=ALU.mult, op1=ALU.mult)
        INC(v.drain(), "lnv")

    def ln_sc(sc, n, src, dst, sqbuf, Blnv, Blns):
        W(sc, "lnv", Blnv + 3 * n + 1)
        INC(sc.activation(out=sqbuf, in_=src, func=AF.Square,
                          accum_out=st1[:, 1:2]), "lns")
        W(sc, "lnv", Blnv + 3 * n + 2)
        INC(sc.activation(out=st1[:, 7:8], in_=st1[:, 6:7], func=AF.Sqrt),
            "lns")
        W(sc, "lnv", Blnv + 3 * n + 3)
        INC(sc.activation(out=dst, in_=src, func=AF.Identity, scale=st1[:, 8:9],
                          bias=st1[:, 9:10]), "lna")

    if PH <= 3:
        es.close()
        return nc
    with contextlib.ExitStack() as px:
        sbl = lambda n, s, d: px.enter_context(nc.sbuf_tensor(n, s, d))
        wosl = [sbl(f"wosl{i}", [128, NIT, 512], BF16) for i in range(2)]
        ynld = [sbl(f"ynld{i}", [128, NIT, 128], BF16) for i in range(2)]
        mixst = [sbl(f"mixst{i}", [128, 512], BF16) for i in range(2)]

        B = {k: S[k].n for k in S}
        entry_all()

        @block.sync
        def _(sq, B=B):
            for dc in range(8):
                if dc >= 2:
                    W(sq, "opd", B["opd"] + NTT * (dc - 1))
                src = bass.AP(woT_ag.tensor, woT_ag.offset + dc * 512,
                              [[DM, 128], [128 * DM, NIT], [1, 512]])
                INC(sq.dma_start(out=wosl[dc % 2][:], in_=src),
                    "wol0" if dc % 2 == 0 else "wol1", 16)
                for tt in range(NTT):
                    n = dc * NTT + tt
                    if n >= 2:
                        W(sq, "opd", B["opd"] + n - 1)
                    INC(sq.dma_start(out=ynld[n % 2][:],
                                     in_=yn_sp[tt].transpose([1, 0, 2])),
                        "ynl0" if n % 2 == 0 else "ynl1", 16)
            bar(sq, "sync")

        @block.tensor
        def _(te, B=B):
            W(te, "ccs", 2)
            for dc in range(8):
                W(te, "wol0" if dc % 2 == 0 else "wol1",
                  B["wol0" if dc % 2 == 0 else "wol1"] + 16 * (dc // 2 + 1))
                for tt in range(NTT):
                    n = dc * NTT + tt
                    W(te, "ynl0" if n % 2 == 0 else "ynl1",
                      B["ynl0" if n % 2 == 0 else "ynl1"] + 16 * (n // 2 + 1))
                    if n >= 2:
                        W(te, "mxs", B["mxs"] + n - 1)
                    for it in range(NIT):
                        mm = te.matmul(out=pA[n % 2][:, 0:512],
                                       lhsT=ynld[n % 2][:, it, :],
                                       rhs=wosl[dc % 2][:, it, :],
                                       start=(it == 0), stop=(it == NIT - 1))
                    INC(mm, "opd")
            bar(te, "tensor")

        @block.vector
        def _(v, B=B):
            for n in range(8 * NTT):
                W(v, "opd", B["opd"] + n + 1)
                if n >= 2:
                    W(v, "spm" if n % 2 == 0 else "spm1",
                      B["spm" if n % 2 == 0 else "spm1"] + 16 * (n // 2))
                INC(v.tensor_copy(out=mixst[n % 2][:], in_=pA[n % 2][:, 0:512]),
                    "mxs")
            bar(v, "vector")

        @block.gpsimd
        def _(gq, B=B):
            for dc in range(8):
                for tt in range(NTT):
                    n = dc * NTT + tt
                    W(gq, "mxs", B["mxs"] + n + 1)
                    INC(gq.dma_start(out=mix_sp[tt][:, dc * 512:(dc + 1) * 512],
                                     in_=mixst[n % 2][:]),
                        "spm" if n % 2 == 0 else "spm1", 16)
            bar(gq, "gpsimd")

        @block.scalar
        def _(sc):
            bar(sc, "scalar")

    if PH <= 4:
        es.close()
        return nc
    with contextlib.ExitStack() as px:
        sbl = lambda n, s, d: px.enter_context(nc.sbuf_tensor(n, s, d))
        mix_t = sbl("mix_t", [128, DM], BF16)
        xt2 = [sbl("xt2s", [128, DM], F32)] * 2
        lnm_t = sbl("lnm_t", [128, DM], F32)
        x2_t = sbl("x2_t", [128, DM], F32)
        xn2_b = sbl("xn2_b", [128, DM], BF16)
        sq_b4 = sbl("sq_b4", [128, DM], BF16)
        xfstg = sbl("xfstg", [128, NDT, 128], BF16)

        B = {k: S[k].n for k in S}
        entry_all()

        @block.sync
        def _(sq, B=B):
            for tt in range(NTT):
                if tt >= 1:
                    W(sq, "lna", B["lna"] + 2 * tt)
                    W(sq, "x2s", B["x2s"] + tt)
                INC(sq.dma_start(out=mix_t[:], in_=mix_sp[tt]),
                    "xld0" if tt % 2 == 0 else "xld1", 16)
                INC(sq.dma_start(out=xt2[tt % 2][:],
                                 in_=x_in[tt * 128:(tt + 1) * 128]),
                    "xld0" if tt % 2 == 0 else "xld1", 16)
            bar(sq, "sync")

        @block.vector
        def _(v, B=B):
            for tt in range(NTT):
                W(v, "xld0" if tt % 2 == 0 else "xld1",
                  B["xld0" if tt % 2 == 0 else "xld1"] + 32 * (tt // 2 + 1))
                ln_v(v, 2 * tt, mix_t[:], B["lnv"], B["lns"])
                W(v, "lna", B["lna"] + 2 * tt + 1)
                if tt >= 1:
                    W(v, "xfs", B["xfs"] + 16 * tt)
                v.tensor_tensor(out=x2_t[:], in0=xt2[tt % 2][:], in1=lnm_t[:],
                                op=ALU.add)
                INC(v.drain(), "x2s")
                ln_v(v, 2 * tt + 1, x2_t[:], B["lnv"], B["lns"])
                if tt >= 1:
                    W(v, "xfs2", B["xfs2"] + 16 * tt)
                for dt in range(NDT):
                    W(v, "tp", B["tp"] + tt * NDT + dt + 1)
                    INC(v.tensor_copy(out=xfstg[:, dt, :],
                                      in_=p_tb[(tt * NDT + dt) % 2][:]), "ev")
            bar(v, "vector")

        @block.scalar
        def _(sc, B=B):
            for tt in range(NTT):
                ln_sc(sc, 2 * tt, mix_t[:], lnm_t[:], sq_b4[:], B["lnv"],
                      B["lns"])
                W(sc, "x2s", B["x2s"] + tt + 1)
                if tt >= 1:
                    W(sc, "tp", B["tp"] + tt * NDT)
                ln_sc(sc, 2 * tt + 1, x2_t[:], xn2_b[:], sq_b4[:], B["lnv"],
                      B["lns"])
            bar(sc, "scalar")

        @block.tensor
        def _(te, B=B):
            for tt in range(NTT):
                W(te, "lna", B["lna"] + 2 * tt + 2)
                for dt in range(NDT):
                    n = tt * NDT + dt
                    if n >= 2:
                        W(te, "ev", B["ev"] + n - 1)
                    INC(te.transpose(out=p_tb[n % 2][:],
                                     in_=xn2_b[:, dt * 128:(dt + 1) * 128],
                                     identity=id_b[:]), "tp")
            bar(te, "tensor")

        @block.gpsimd
        def _(gq, B=B):
            for tt in range(NTT):
                W(gq, "x2s", B["x2s"] + tt + 1)
                INC(gq.dma_start(out=x2_sp[tt], in_=x2_t[:]), "xfs", 16)
                W(gq, "ev", B["ev"] + 32 * (tt + 1))
                INC(gq.dma_start(out=xf_sp[tt].transpose([1, 0, 2]),
                                 in_=xfstg[:]), "xfs2", 16)
            bar(gq, "gpsimd")

    if PH <= 5:
        es.close()
        return nc
    with contextlib.ExitStack() as px:
        sbl = lambda n, s, d: px.enter_context(nc.sbuf_tensor(n, s, d))
        wfsl = [sbl("wfsl0", [128, NDT, 512], BF16)] * 2
        xfld = [sbl(f"xfld{i}", [128, NDT, 128], BF16) for i in range(2)]
        yfst = [sbl(f"yfst{i}", [128, 512], BF16) for i in range(2)]
        fb_s = sbl("fb_s", [1, DM], F32)
        fbb = sbl("fbb", [128, 512], F32)

        B = {k: S[k].n for k in S}
        entry_all()

        @block.tensor
        def _(te, B=B):
            W(te, "ccs", 3)
            W(te, "grl", 32)
            for oc in range(8):
                if oc >= 1:
                    W(te, "mxs", B["mxs"] + NTT * oc)
                INC(te.matmul(out=pb8[:, 0:512], lhsT=ones1f[0:1, 0:128],
                              rhs=fb_s[0:1, oc * 512:(oc + 1) * 512],
                              start=True, stop=True), "opd")
                W(te, "ffl0" if oc % 2 == 0 else "ffl1",
                  B["ffl0" if oc % 2 == 0 else "ffl1"] + 16 * (oc // 2 + 1))
                for tt in range(NTT):
                    n = oc * NTT + tt
                    W(te, "yfl" if n % 2 == 0 else "yfl1",
                      B["yfl" if n % 2 == 0 else "yfl1"] + 16 * (n // 2 + 1))
                    if n >= 2:
                        W(te, "mxs", B["mxs"] + n - 1)
                    for it in range(NDT):
                        mm = te.matmul(out=pA[n % 2][:, 0:512],
                                       lhsT=xfld[n % 2][:, it, :],
                                       rhs=wfsl[oc % 2][:, it, :],
                                       start=(it == 0), stop=(it == NDT - 1))
                    INC(mm, "ffd")
            bar(te, "tensor")

        @block.sync
        def _(sq, B=B):
            INC(sq.dma_start(out=fb_s[:], in_=fb_in), "grl", 16)
            for oc in range(8):
                if oc >= 1:
                    W(sq, "ffd", B["ffd"] + NTT * oc)
                cidx, off = (oc * 512) // OPC, (oc * 512) % OPC
                src = bass.AP(wlT_ag.tensor,
                              wlT_ag.offset + cidx * NIT * 128 * OPC + off,
                              [[OPC, 128], [128 * OPC, NDT], [1, 512]])
                INC(sq.dma_start(out=wfsl[oc % 2][:], in_=src),
                    "ffl0" if oc % 2 == 0 else "ffl1", 16)
                for tt in range(NTT):
                    n = oc * NTT + tt
                    if n >= 2:
                        W(sq, "ffd", B["ffd"] + n - 1)
                    INC(sq.dma_start(out=xfld[n % 2][:],
                                     in_=xf_sp[tt].transpose([1, 0, 2])),
                        "yfl" if n % 2 == 0 else "yfl1", 16)
            bar(sq, "sync")

        @block.vector
        def _(v, B=B):
            for n in range(8 * NTT):
                W(v, "ffd", B["ffd"] + n + 1)
                if n % NTT == 0:
                    W(v, "fbc", B["fbc"] + n // NTT + 1)
                if n >= 2:
                    W(v, "spm" if n % 2 == 0 else "spm1",
                      B["spm" if n % 2 == 0 else "spm1"] + 16 * (n // 2))
                INC(v.tensor_tensor(out=yfst[n % 2][:], in0=pA[n % 2][:, 0:512],
                                    in1=fbb[:], op=ALU.add), "mxs")
            bar(v, "vector")

        @block.gpsimd
        def _(gq, B=B):
            for oc in range(8):
                for tt in range(NTT):
                    n = oc * NTT + tt
                    W(gq, "mxs", B["mxs"] + n + 1)
                    INC(gq.dma_start(out=yf_sp[tt][:, oc * 512:(oc + 1) * 512],
                                     in_=yfst[n % 2][:]),
                        "spm" if n % 2 == 0 else "spm1", 16)
            bar(gq, "gpsimd")

        @block.scalar
        def _(sc, B=B):
            for oc in range(8):
                W(sc, "opd", B["opd"] + oc + 1)
                INC(sc.activation(out=fbb[:], in_=pb8[:, 0:512],
                                  func=AF.Copy), "fbc")
            bar(sc, "scalar")

    if PH <= 6:
        es.close()
        return nc
    with contextlib.ExitStack() as px:
        sbl = lambda n, s, d: px.enter_context(nc.sbuf_tensor(n, s, d))
        yff_t = [sbl(f"yff_t{i}", [128, DM], BF16) for i in range(2)]
        x2b_t = sbl("x2b_t", [128, DM], F32)
        out_t = [sbl("out_ts", [128, DM], F32)] * 2
        lnm6 = sbl("lnm6", [128, DM], F32)
        sq6 = sbl("sq6", [128, DM], BF16)

        B = {k: S[k].n for k in S}
        entry_all()

        @block.sync
        def _(sq, B=B):
            for tt in range(NTT):
                if tt >= 2:
                    W(sq, "lna", B["lna"] + tt - 1)
                INC(sq.dma_start(out=yff_t[tt % 2][:], in_=yf_sp[tt]),
                    "xld0" if tt % 2 == 0 else "xld1", 16)
                if tt >= 1:
                    W(sq, "x2s", B["x2s"] + tt)
                INC(sq.dma_start(out=x2b_t[:], in_=x2_sp[tt]),
                    "xld0" if tt % 2 == 0 else "xld1", 16)
            W(sq, "od0", B["od0"] + 16 * NTT)

        @block.vector
        def _(v, B=B):
            for tt in range(NTT):
                W(v, "xld0" if tt % 2 == 0 else "xld1",
                  B["xld0" if tt % 2 == 0 else "xld1"] + 32 * (tt // 2 + 1))
                ln_v(v, tt, yff_t[tt % 2][:], B["lnv"], B["lns"])
                W(v, "lna", B["lna"] + tt + 1)
                if tt >= 1:
                    W(v, "od0", B["od0"] + 16 * tt)
                v.tensor_tensor(out=out_t[tt % 2][:], in0=x2b_t[:],
                                in1=lnm6[:], op=ALU.add)
                INC(v.drain(), "x2s")

        @block.scalar
        def _(sc, B=B):
            for tt in range(NTT):
                ln_sc(sc, tt, yff_t[tt % 2][:], lnm6[:], sq6[:], B["lnv"],
                      B["lns"])

        @block.gpsimd
        def _(gq, B=B):
            for tt in range(NTT):
                W(gq, "x2s", B["x2s"] + tt + 1)
                INC(gq.dma_start(out=out[tt * 128:(tt + 1) * 128],
                                 in_=out_t[tt % 2][:]), "od0", 16)

        @block.tensor
        def _(te):
            pass

    es.close()
    return nc


def kernel(**inputs):
    TOK, NCORES = 2048, 8
    in_maps = _host_prepare(inputs, TOK, NCORES)
    nc = build_program(TOK, NCORES)
    res = run_bass_kernel_spmd(nc, in_maps, core_ids=list(range(NCORES)))
    Bsz, L, _ = inputs["L0_attn_hidden_states"].shape
    o = np.zeros((Bsz, L, DM), np.float32)
    for c in range(NCORES):
        b, hh = divmod(c, 2)
        o[b, hh * TOK:(hh + 1) * TOK] = res.results[c]["out"]
    return o


# version 10
# speedup vs baseline: 1.0350x; 1.0350x over previous
import contextlib
import ml_dtypes
import numpy as np

import concourse.bass as bass
import concourse.mybir as mybir
from concourse.bass_utils import run_bass_kernel_spmd

F32 = mybir.dt.float32
BF16 = mybir.dt.bfloat16
U8 = mybir.dt.uint8
ALU = mybir.AluOpType
AF = mybir.ActivationFunctionType
AX = mybir.AxisListType

H, P, N, G, KC = 64, 64, 128, 8, 4
DM = 4096
I = H * P
GN = G * N
CONV_DIM = I + 2 * GN
PROJ = I + CONV_DIM + H
EPS = 1e-5
QCH = 128
NCT = 81
NDT = 32
NIT = 32
NXT = 48
HPG = H // G


def _host_prepare(inputs, TOK, NCORES):
    x = np.ascontiguousarray(inputs["L0_attn_hidden_states"], dtype=np.float32)
    Bsz, L, _ = x.shape
    w_in = inputs["L0_attn_in_proj_w"].astype(np.float32)
    w_out = inputs["L0_attn_out_proj_w"].astype(np.float32)
    qw = inputs["L0_ffn_qweight"]
    lut4 = inputs["L0_ffn_lut4"].astype(np.float32)

    CTS = -(-NCT // NCORES)
    OPC = DM // NCORES
    NOT = OPC // 128
    ORC = DM // NCORES

    wiT = np.zeros((NCT * 128, DM), np.float32)
    wiT[:PROJ, :] = w_in
    t = wiT.reshape(NCT, 128, NDT, 128)
    wi_tiled = np.ascontiguousarray(t.transpose(0, 3, 2, 1))
    wi_shards = []
    for c in range(NCORES):
        sh = np.zeros((CTS, 128, NDT, 128), np.float32)
        lo, hi = CTS * c, min(CTS * c + CTS, NCT)
        if hi > lo:
            sh[:hi - lo] = wi_tiled[lo:hi]
        wi_shards.append(sh)

    woT = np.ascontiguousarray(w_out.T)
    wo_shards = [np.ascontiguousarray(woT[c * ORC:(c + 1) * ORC])
                 for c in range(NCORES)]

    shifts = np.arange(32, dtype=np.int32)
    bit_shards, lut_shards = [], []
    for c in range(NCORES):
        sl = slice(c * OPC, (c + 1) * OPC)
        bs = np.zeros((NOT, 128, 4, DM), np.uint8)
        for p in range(4):
            b = ((qw[p][sl][:, :, None] >> shifts) & 1).astype(np.uint8)
            bs[:, :, p, :] = b.reshape(NOT, 128, DM)
        bit_shards.append(np.ascontiguousarray(bs))
        lut_shards.append(np.ascontiguousarray(
            lut4[sl].reshape(NOT, 128, 16).transpose(1, 0, 2)))

    x_shards, xh_shards = [], []
    for c in range(NCORES):
        b, hh = divmod(c, 2)
        t0 = b * L + hh * TOK
        x_shards.append(np.ascontiguousarray(x.reshape(Bsz * L, DM)[t0:t0 + TOK]))
        xh = np.zeros((128, DM), np.float32)
        if hh == 1:
            xh[0:3] = x.reshape(Bsz * L, DM)[t0 - 3:t0]
        xh_shards.append(xh)

    dvec = inputs["L0_attn_D"].astype(np.float32)
    d_tiled = np.zeros((128, NIT), np.float32)
    for it in range(NIT):
        d_tiled[0:64, it] = dvec[2 * it]
        d_tiled[64:128, it] = dvec[2 * it + 1]
    gnsel = np.zeros((128, NIT, 8), np.float32)
    gnselt = np.zeros((8, NIT, 128), np.float32)
    for it in range(NIT):
        gnsel[:, it, it // 4] = 1.0
        gnselt[it // 4, it, :] = 1.0

    consts = dict(
        conv_w=np.ascontiguousarray(
            inputs["L0_attn_conv_weight"].astype(np.float32).reshape(NXT, 128, KC)
            .transpose(1, 0, 2)),
        conv_b=np.ascontiguousarray(
            inputs["L0_attn_conv_bias"].astype(np.float32).reshape(NXT, 128).T),
        dt_bias=inputs["L0_attn_dt_bias"].astype(np.float32).reshape(64, 1),
        a_log=inputs["L0_attn_A_log"].astype(np.float32).reshape(64, 1),
        d_tiled=d_tiled,
        nw=np.ascontiguousarray(
            inputs["L0_attn_norm_weight"].astype(np.float32).reshape(NIT, 128).T),
        ffn_bias=inputs["L0_ffn_bias"].astype(np.float32).reshape(1, DM),
        ident=np.eye(128, dtype=np.float32),
        tri01=np.triu(np.ones((128, 128), np.float32)),
        gnsel=gnsel,
        gnselt=gnselt.astype(ml_dtypes.bfloat16),
    )
    in_maps = []
    for c in range(NCORES):
        in_maps.append(dict(
            x_in=x_shards[c], xhalo=xh_shards[c], wi_sl=wi_shards[c],
            wo_sl=wo_shards[c], bits=bit_shards[c], lut4=lut_shards[c],
            st_scale=np.full((128, 1), 1.0 if (c % 2) else 0.0, np.float32),
            **consts))
    return in_maps


def build_program(TOK, NCORES):
    import os
    PH = int(os.environ.get("KPH", "99"))
    nc = bass.Bass("TRN2", target_bir_lowering=False)
    NCH = TOK // QCH
    NTT = TOK // 128
    TOKH = TOK // 2
    NTC = max(1, TOKH // 512)
    TCH = TOKH // NTC
    NTTH = TOKH // 128
    CTS = -(-NCT // NCORES)
    OPC = DM // NCORES
    NOT = OPC // 128
    ORC = DM // NCORES
    NWOC = ORC // 128
    HCH = H * NCH
    ISTRIP = 1024
    NSTRIP = DM // ISTRIP
    NWOSL = I // 512

    AGS = {"addr_space": "Shared"} if NCORES > 4 else {}
    x_in = nc.dram_tensor("x_in", [TOK, DM], F32, kind="ExternalInput").ap()
    xhalo = nc.dram_tensor("xhalo", [128, DM], F32, kind="ExternalInput").ap()
    wi_sl = nc.dram_tensor("wi_sl", [CTS, 128, NDT, 128], F32,
                           kind="ExternalInput").ap()
    wo_sl = nc.dram_tensor("wo_sl", [ORC, DM], F32, kind="ExternalInput").ap()
    bits_in = nc.dram_tensor("bits", [NOT, 128, 4, DM], U8,
                             kind="ExternalInput").ap()
    lut4_in = nc.dram_tensor("lut4", [128, NOT, 16], F32, kind="ExternalInput").ap()
    convw_in = nc.dram_tensor("conv_w", [128, NXT, KC], F32,
                              kind="ExternalInput").ap()
    convb_in = nc.dram_tensor("conv_b", [128, NXT], F32, kind="ExternalInput").ap()
    dtb_in = nc.dram_tensor("dt_bias", [64, 1], F32, kind="ExternalInput").ap()
    alog_in = nc.dram_tensor("a_log", [64, 1], F32, kind="ExternalInput").ap()
    dtl_in = nc.dram_tensor("d_tiled", [128, NIT], F32, kind="ExternalInput").ap()
    nw_in = nc.dram_tensor("nw", [128, NIT], F32, kind="ExternalInput").ap()
    fb_in = nc.dram_tensor("ffn_bias", [1, DM], F32, kind="ExternalInput").ap()
    id_in = nc.dram_tensor("ident", [128, 128], F32, kind="ExternalInput").ap()
    tri_in = nc.dram_tensor("tri01", [128, 128], F32, kind="ExternalInput").ap()
    gne_in = nc.dram_tensor("gnsel", [128, NIT, 8], F32, kind="ExternalInput").ap()
    gnt_in = nc.dram_tensor("gnselt", [8, NIT, 128], BF16,
                            kind="ExternalInput").ap()
    ssc_in = nc.dram_tensor("st_scale", [128, 1], F32, kind="ExternalInput").ap()

    out = nc.dram_tensor("out", [TOK, DM], F32, kind="ExternalOutput").ap()

    wiT_in = nc.dram_tensor("wiT_in", [CTS, 128, NDT, 128], BF16,
                            kind="Internal").ap()
    wiT_ag = nc.dram_tensor("wiT_ag", [NCORES * CTS, 128, NDT, 128], BF16,
                            kind="Internal", **AGS).ap()
    woT_in = nc.dram_tensor("woT_in", [ORC, DM], BF16, kind="Internal").ap()
    woT_ag = nc.dram_tensor("woT_ag", [I, DM], BF16, kind="Internal", **AGS).ap()
    wlT_in = nc.dram_tensor("wlT_in", [NIT, 128, OPC], BF16, kind="Internal").ap()
    wlT_ag = nc.dram_tensor("wlT_ag", [NCORES, NIT, 128, OPC], BF16,
                            kind="Internal", **AGS).ap()
    st_in = nc.dram_tensor("st_in", [128, H * 64], F32, kind="Internal").ap()
    lfl_d = nc.dram_tensor("lfl_d", [64, TOK], F32, kind="Internal").ap()
    cfl_d = nc.dram_tensor("cfl_d", [64, TOK], F32, kind="Internal").ap()
    st_ag = nc.dram_tensor("st_ag", [2, 128, H * 64], F32, kind="Internal").ap()

    xbc_sp = nc.dram_tensor("xbc_sp", [NCH, NXT, 128, QCH], BF16,
                            kind="Internal").ap()
    gate_sp = nc.dram_tensor("gate_sp", [NCH, NIT, 128, QCH], BF16,
                             kind="Internal").ap()
    yn_sp = nc.dram_tensor("yn_sp", [NCH, NIT, 128, QCH], BF16,
                           kind="Internal").ap()
    mix_sp = nc.dram_tensor("mix_sp", [NTT, 128, DM], BF16, kind="Internal").ap()
    x2_sp = nc.dram_tensor("x2_sp", [NTT, 128, DM], F32, kind="Internal").ap()
    xf_sp = nc.dram_tensor("xf_sp", [TOK // 128, NDT, 128, 128], BF16,
                           kind="Internal").ap()
    yf_sp = nc.dram_tensor("yf_sp", [NTT, 128, DM], BF16, kind="Internal").ap()

    es = contextlib.ExitStack()
    sb = lambda n, s, d: es.enter_context(nc.sbuf_tensor(n, s, d))
    ps = lambda n, s, d=F32: es.enter_context(nc.psum_tensor(n, s, d))

    xnfm = sb("xnfm", [128, NDT, TOKH + 4], BF16)
    id_b = sb("id_b", [128, 128], BF16)
    id_f = sb("id_f", [128, 128], F32)
    tri_b = sb("tri_b", [128, 128], BF16)
    gne_b = sb("gne_b", [128, NIT, 8], BF16)
    convw_s = sb("convw_s", [128, NXT, KC], F32)
    convb_s = sb("convb_s", [128, NXT], F32)
    dtb_s = sb("dtb_s", [64, 1], F32)
    alog_s = sb("alog_s", [64, 1], F32)
    nega_s = sb("nega_s", [64, 1], F32)
    dtl_s = sb("dtl_s", [128, NIT], F32)
    nw_s = sb("nw_s", [128, NIT], F32)
    ssc_s = sb("ssc_s", [128, 1], F32)
    ones1f = sb("ones1f", [1, 128], F32)
    ones65 = sb("ones65", [65, 128], F32)
    zer128 = sb("zer128", [128, 128], F32)
    dt_fm = sb("dt_fm", [64, TOK], F32)
    lloc = sb("lloc", [64, TOK], F32)
    cum_a = sb("cum_a", [64, TOK], F32)
    rev_a = sb("rev_a", [64, TOK], F32)
    dect = sb("dect", [64, NCH], F32)
    dkfl = sb("dkfl", [1, HCH], F32)
    dkbc = sb("dkbc", [128, HCH], F32)
    s_f = sb("s_f", [128, H * 64], F32)
    st1 = sb("st1", [128, 12], F32)

    pA = [ps(f"pA{i}", [128, 512]) for i in range(2)]
    pb4 = ps("pb4", [128, 512])
    pb5 = ps("pb5", [128, 512])
    pb7 = ps("pb7", [128, 512])
    pb8 = ps("pb8", [128, 512])
    pbB = ps("pbB", [128, 1024], BF16)
    pbB2 = ps("pbB2", [128, 1024], BF16)
    p_tf = [pb7[:, 0:128], pb7[:, 128:256]]
    p_g = pA[0][:, 128:256]
    p_h3 = pb4[:, 320:328]
    p_gn = pA[1][:, 128:136]
    p_lb = [pb4[:, 0:128], pb5[:, 0:128]]
    p_cu = [pb4[:, 128:256], pb5[:, 128:256]]
    p_y = [pA[0][:, 0:128], pA[1][:, 0:128]]
    p_rs = [pb7[:, 256:384], pb8[:, 0:128]]
    p_x = [pb4[:, 256:320], pb5[:, 256:320]]
    p_tb = [pbB[:, 0:128], pbB2[:, 0:128]]
    p_tc = [pbB[:, 128:256], pbB2[:, 128:256]]

    class Sem:
        __slots__ = ("h", "n")
        def __init__(self, name):
            self.h = es.enter_context(nc.semaphore(name))
            self.n = 0
    S = {}
    for nm in ("ld0 gq0 cld0 cdone wb0 wbl agr bld0 bitc deqv tp ev ccs ph "
               "vc xld0 xld1 lnv lns lna wld0 wld1 mmr evc scg scx cvv spg "
               "spx dtq dts lf0 xrl0 xrl1 tp2 ue2 vb2 pex st2 cq sbc gev "
               "dsv exq mtq cpq yev sqq gnm rsq ynd wol0 wol1 opd "
               "mxs spm ynl0 ynl1 mxl x2s xfs ffl0 ffl1 ffd yfs yfl outd "
               "grl gld0 gld1 ue2b vb2b ut2 tpf tpb rsm spy ymd dsv2 "
               "scT sti spg1 spx1 spm1 od0 od1 xfs2 yfl1 fbc").split():
        S[nm] = Sem(nm)

    def INC(instr, nm, k=1):
        instr.then_inc(S[nm].h, k)
        S[nm].n += k
        return instr

    def W(eng, nm, val):
        eng.wait_ge(S[nm].h, val)

    for nm in ("arr_sync", "arr_vector", "arr_scalar", "arr_tensor", "go"):
        S[nm] = Sem(nm)

    block = es.enter_context(nc.Block())
    bark = {"sync": 0, "vector": 0, "scalar": 0, "gpsimd": 0, "tensor": 0}

    def bar(eng, name):
        bark[name] += 1
        k = bark[name]
        if name == "gpsimd":
            for other in ("sync", "vector", "scalar", "tensor"):
                eng.wait_ge(S["arr_" + other].h, k)
            eng.sem_inc(S["go"].h, 1)
        else:
            eng.sem_inc(S["arr_" + name].h, 1)
            eng.wait_ge(S["go"].h, k)

    def _entry(eng):
        for nm, s in S.items():
            if s.n > 0 and nm not in ("ph", "go") and not nm.startswith("arr_"):
                eng.wait_ge(s.h, s.n)

    def entry_all():
        @block.sync
        def _(sq):
            _entry(sq)
        @block.vector
        def _(v):
            _entry(v)
        @block.scalar
        def _(sc):
            _entry(sc)
        @block.tensor
        def _(te):
            _entry(te)
        @block.gpsimd
        def _(gq):
            _entry(gq)

    if PH <= -1:
        es.close()
        return nc
    KP0 = int(os.environ.get("KP0", "4"))
    KP3 = int(os.environ.get("KP3", "4"))
    KP2 = int(os.environ.get("KP2", "2"))
    KNEU = os.environ.get("KNEU", "")
    KNR = os.environ.get("KNR", "")
    NCAST = CTS + NWOC
    NITS = ISTRIP // 128

    with contextlib.ExitStack() as px:
        sbl = lambda n, s, d: px.enter_context(nc.sbuf_tensor(n, s, d))
        c_f = sbl("c_f", [128, NDT * 128], F32)
        c_b = sbl("c_b", [128, NDT * 128], BF16)
        lut4_t = sbl("lut4_t", [128, NOT, 16], F32)
        lutd = sbl("lutd", [128, NOT, 8], F32)
        bitu = sbl("bitu", [128, 4, DM], U8)
        bitb = sbl("bitb", [128, 1, ISTRIP], BF16)
        arm = [sbl(f"arm{i}", [128, ISTRIP], BF16) for i in range(8)]
        gne_f = sbl("gne_f", [128, NIT, 8], F32)
        tri_f = sbl("tri_f", [128, 128], F32)
        wlT_sb = sbl("wlT_sb", [128, NIT, 128], BF16)

        @block.sync
        def _(sq):
            for dst, src in [(lut4_t[:], lut4_in), (convw_s[:], convw_in),
                             (convb_s[:], convb_in), (dtb_s[:], dtb_in),
                             (alog_s[:], alog_in), (dtl_s[:], dtl_in),
                             (nw_s[:], nw_in),
                             (id_f[:], id_in), (tri_f[:], tri_in),
                             (gne_f[:], gne_in),
                             (ssc_s[:], ssc_in)]:
                INC(sq.dma_start(out=dst, in_=src), "ld0", 16)
            if KP0 >= 2:
                for j in range(NCAST):
                    if j >= 1:
                        W(sq, "cdone", j)
                    src = (wi_sl[j] if j < CTS
                           else wo_sl[(j - CTS) * 128:(j - CTS + 1) * 128])
                    INC(sq.dma_start(out=c_f[:], in_=src), "cld0", 16)
            if KP0 >= 3:
                for ot in range(NOT):
                    if ot >= 1 and KP3 >= 2:
                        W(sq, "bitc", ot * NSTRIP)
                    INC(sq.dma_start(out=bitu[:], in_=bits_in[ot]), "bld0", 16)
            bar(sq, "sync")

        @block.gpsimd
        def _(gq):
            INC(gq.memset(zer128[:], 0.0), "gq0")
            INC(gq.memset(ones1f[:], 1.0), "gq0")
            gq.memset(ones65[:], 1.0)
            INC(gq.memset(s_f[:], 0.0), "gq0")
            if KP0 >= 4:
                W(gq, "agr", 1)
                INC(gq.collective_compute(
                    "AllGather", ALU.bypass, replica_groups=[list(range(NCORES))],
                    ins=[wiT_in[:]], outs=[wiT_ag[:]]), "ccs")
                W(gq, "agr", 2)
                INC(gq.collective_compute(
                    "AllGather", ALU.bypass, replica_groups=[list(range(NCORES))],
                    ins=[woT_in[:]], outs=[woT_ag[:]]), "ccs")
                W(gq, "agr", 3)
                INC(gq.collective_compute(
                    "AllGather", ALU.bypass, replica_groups=[list(range(NCORES))],
                    ins=[wlT_in[:]], outs=[wlT_ag[:]]), "ccs")
            bar(gq, "gpsimd")

        @block.vector
        def _(v):
            W(v, "ld0", 11 * 16)
            v.tensor_copy(out=id_b[:], in_=id_f[:])
            v.tensor_copy(out=tri_b[:], in_=tri_f[:])
            v.tensor_copy(out=gne_b[:], in_=gne_f[:])
            INC(v.tensor_tensor(out=lutd[:], in0=lut4_t[:, :, 1::2],
                                in1=lut4_t[:, :, 0::2], op=ALU.subtract), "vc")
            for j in range(NCAST if KP0 >= 2 else 0):
                W(v, "cld0", 16 * (j + 1))
                if j >= 1:
                    W(v, "wb0", 16 * j)
                INC(v.tensor_copy(out=c_b[:], in_=c_f[:]), "cdone")
            for ot in range(NOT if KP0 >= 3 else 0):
                if KP3 < 2:
                    break
                W(v, "bld0", 16 * (ot + 1))
                for st in range(NSTRIP):
                    nst = ot * NSTRIP + st
                    sl = slice(st * ISTRIP, (st + 1) * ISTRIP)
                    v.tensor_copy(out=bitb[:, 0], in_=bitu[:, 0, sl])
                    v.drain()
                    for c in range(8):
                        v.tensor_scalar(out=arm[c][:], in0=bitb[:, 0],
                                        scalar1=lutd[:, ot, c:c + 1],
                                        scalar2=lut4_t[:, ot, 2 * c:2 * c + 1],
                                        op0=ALU.mult, op1=ALU.add)
                    v.drain()
                    for c in range(4):
                        v.copy_predicated(out=arm[2 * c][:],
                                          mask=bitu[:, 1, sl],
                                          data=arm[2 * c + 1][:])
                    v.drain()
                    v.copy_predicated(out=arm[0][:], mask=bitu[:, 2, sl],
                                      data=arm[2][:])
                    v.copy_predicated(out=arm[4][:], mask=bitu[:, 2, sl],
                                      data=arm[6][:])
                    v.drain()
                    INC(v.copy_predicated(out=arm[0][:], mask=bitu[:, 3, sl],
                                          data=arm[4][:]), "bitc")
                    for it in range(NITS if KP3 >= 3 else 0):
                        W(v, "tp", nst * NITS + it + 1)
                        itg = st * NITS + it
                        INC(v.tensor_copy(
                            out=wlT_sb[:, itg, :],
                            in_=p_tb[(nst * NITS + it) % 2][:]), "ev")
            bar(v, "vector")

        @block.scalar
        def _(sc):
            for j in range(NCAST if KP0 >= 2 else 0):
                W(sc, "cdone", j + 1)
                dst = (wiT_in[j] if j < CTS
                       else woT_in[(j - CTS) * 128:(j - CTS + 1) * 128])
                INC(sc.dma_start(out=dst, in_=c_b[:]), "wb0", 16)
                if j == CTS - 1:
                    W(sc, "wb0", 16 * CTS)
                    sc.sem_inc(S["agr"].h, 1)
                if j == NCAST - 1:
                    W(sc, "wb0", 16 * NCAST)
                    sc.sem_inc(S["agr"].h, 1)
            for ot in range(NOT if (KP0 >= 3 and KP3 >= 4) else 0):
                W(sc, "ev", (ot + 1) * NIT)
                INC(sc.dma_start(
                    out=wlT_in[:, :, ot * 128:(ot + 1) * 128]
                    .transpose([1, 0, 2]), in_=wlT_sb[:]), "wbl", 16)
            if KP0 >= 3 and KP3 >= 4:
                W(sc, "wbl", 16 * NOT)
                sc.sem_inc(S["agr"].h, 1)
            bar(sc, "scalar")

        @block.tensor
        def _(te):
            W(te, "vc", 1)
            ntp = 0
            for ot in range(NOT if (KP0 >= 3 and KP3 >= 3) else 0):
                for st in range(NSTRIP):
                    W(te, "bitc", ot * NSTRIP + st + 1)
                    if ot >= 1 and st == 0 and KP3 >= 4:
                        W(te, "wbl", 16 * ot)
                    for it in range(NITS):
                        if ntp >= 2:
                            W(te, "ev", ntp - 1)
                        INC(te.transpose(out=p_tb[ntp % 2][:],
                                         in_=arm[0][:, it * 128:(it + 1) * 128],
                                         identity=id_b[:]), "tp")
                        ntp += 1
            bar(te, "tensor")

    if PH <= 0:
        es.close()
        return nc
    with contextlib.ExitStack() as px:
        sbl = lambda n, s, d: px.enter_context(nc.sbuf_tensor(n, s, d))
        xt = [sbl(f"xt{i}", [128, DM], F32) for i in range(2)]
        xn_b = [sbl(f"xn_b{i}", [128, DM], BF16) for i in range(2)]
        wsl = [sbl(f"wsl{i}", [128, NDT * 128], BF16) for i in range(2)]
        hstg = [sbl("hstg0", [128, TOKH + 3], F32)] * 2
        cacc = sbl("cacc", [128, TOKH], F32)
        cstg = [sbl("cstg0", [128, TOKH], BF16)] * 2
        gstg = [sbl("gstg0", [128, TOKH], BF16)] * 2
        halo_sv = sbl("halo_sv", [128, NXT, 3], F32)
        NCHH = TOKH // QCH

        for hh in range(2):
            NTA = NTTH + (1 if hh == 0 else 0)
            B = {k: S[k].n for k in S}
            entry_all()

            @block.sync
            def _(sq, hh=hh, NTA=NTA, B=B):
                for g in range(NTA):
                    if g >= 2:
                        W(sq, "lna", B["lna"] + g - 1)
                    src = (x_in[(hh * NTTH + g) * 128:(hh * NTTH + g + 1) * 128]
                           if g < NTTH else xhalo)
                    INC(sq.dma_start(out=xt[g % 2][:], in_=src),
                        "xld0" if g % 2 == 0 else "xld1", 16)
                bar(sq, "sync")

            @block.vector
            def _(v, hh=hh, NTA=NTA, B=B):
                nev = 0
                for g in range(NTA):
                    W(v, "xld0" if g % 2 == 0 else "xld1",
                      B["xld0" if g % 2 == 0 else "xld1"] + 16 * (g // 2 + 1))
                    v.tensor_reduce(out=st1[:, 0:1], in_=xt[g % 2][:], axis=AX.X,
                                    op=ALU.add)
                    INC(v.drain(), "lnv")
                    W(v, "lns", B["lns"] + 2 * g + 1)
                    v.tensor_scalar(out=st1[:, 2:3], in0=st1[:, 0:1],
                                    scalar1=1.0 / DM, scalar2=None, op0=ALU.mult)
                    v.drain()
                    v.tensor_tensor(out=st1[:, 3:4], in0=st1[:, 2:3],
                                    in1=st1[:, 2:3], op=ALU.mult)
                    v.tensor_scalar(out=st1[:, 4:5], in0=st1[:, 1:2],
                                    scalar1=1.0 / DM, scalar2=None, op0=ALU.mult)
                    v.drain()
                    v.tensor_tensor(out=st1[:, 5:6], in0=st1[:, 4:5],
                                    in1=st1[:, 3:4], op=ALU.subtract)
                    v.drain()
                    v.tensor_scalar(out=st1[:, 6:7], in0=st1[:, 5:6],
                                    scalar1=EPS, scalar2=None, op0=ALU.add)
                    INC(v.drain(), "lnv")
                    W(v, "lns", B["lns"] + 2 * g + 2)
                    v.reciprocal(out=st1[:, 8:9], in_=st1[:, 7:8])
                    v.drain()
                    v.tensor_scalar(out=st1[:, 9:10], in0=st1[:, 2:3],
                                    scalar1=st1[:, 8:9], scalar2=-1.0,
                                    op0=ALU.mult, op1=ALU.mult)
                    INC(v.drain(), "lnv")
                    ncol = 3 if g == NTTH else 128
                    for dt in range(NDT):
                        W(v, "tp", B["tp"] + g * NDT + dt + 1)
                        dst_c = (slice(TOKH, TOKH + 3) if g == NTTH
                                 else slice(g * 128, (g + 1) * 128))
                        INC(v.tensor_copy(
                            out=xnfm[:, dt, dst_c],
                            in_=p_tb[(g * NDT + dt) % 2][:, 0:ncol]), "ev")
                bar(v, "vector")

            @block.scalar
            def _(sc, hh=hh, NTA=NTA, B=B):
                for g in range(NTA):
                    W(sc, "lnv", B["lnv"] + 3 * g + 1)
                    INC(sc.activation(out=wsl[0][:], in_=xt[g % 2][:],
                                      func=AF.Square, accum_out=st1[:, 1:2]),
                        "lns")
                    W(sc, "lnv", B["lnv"] + 3 * g + 2)
                    INC(sc.activation(out=st1[:, 7:8], in_=st1[:, 6:7],
                                      func=AF.Sqrt), "lns")
                    W(sc, "lnv", B["lnv"] + 3 * g + 3)
                    if g >= 2:
                        W(sc, "tp", B["tp"] + (g - 1) * NDT)
                    INC(sc.activation(out=xn_b[g % 2][:], in_=xt[g % 2][:],
                                      func=AF.Identity, scale=st1[:, 8:9],
                                      bias=st1[:, 9:10]), "lna")
                bar(sc, "scalar")

            @block.tensor
            def _(te, hh=hh, NTA=NTA, B=B):
                for g in range(NTA):
                    W(te, "lna", B["lna"] + g + 1)
                    for dt in range(NDT):
                        n = g * NDT + dt
                        if n >= 2:
                            W(te, "ev", B["ev"] + n - 1)
                        INC(te.transpose(
                            out=p_tb[n % 2][:],
                            in_=xn_b[g % 2][:, dt * 128:(dt + 1) * 128],
                            identity=id_b[:]), "tp")
                bar(te, "tensor")

            @block.gpsimd
            def _(gq):
                bar(gq, "gpsimd")

            B = {k: S[k].n for k in S}
            entry_all()

            @block.sync
            def _(sq, hh=hh, B=B):
                for ct in range(NCT):
                    if ct >= 2:
                        W(sq, "mmr", B["mmr"] + ct - 1)
                    INC(sq.dma_start(out=wsl[ct % 2][:], in_=wiT_ag[ct]),
                        "wld0" if ct % 2 == 0 else "wld1", 16)
                bar(sq, "sync")

            @block.tensor
            def _(te, hh=hh, B=B):
                if hh == 0:
                    W(te, "ccs", 1)
                for ct in range(NCT):
                    W(te, "wld0" if ct % 2 == 0 else "wld1",
                      B["wld0" if ct % 2 == 0 else "wld1"] + 16 * (ct // 2 + 1))
                    if ct >= 1:
                        W(te, "evc", B["evc"] + ct)
                    for dt in range(NDT):
                        lhsT = wsl[ct % 2][:, dt * 128:(dt + 1) * 128]
                        st, sp = dt == 0, dt == NDT - 1
                        for k in range(NTC):
                            mm = te.matmul(
                                out=pA[k][:, 0:TCH], lhsT=lhsT,
                                rhs=xnfm[:, dt, k * TCH:(k + 1) * TCH],
                                start=st, stop=sp)
                        if hh == 0:
                            mm = te.matmul(out=p_h3[:, 0:3], lhsT=lhsT,
                                           rhs=xnfm[:, dt, TOKH:TOKH + 3],
                                           start=st, stop=sp)
                    INC(mm, "mmr")
                bar(te, "tensor")

            @block.vector
            def _(v, hh=hh, B=B):
                for ct in range(NCT):
                    if ct < 32:
                        continue
                    W(v, "mmr", B["mmr"] + ct + 1)
                    if ct < 80:
                        xi = ct - 32
                        v.drain()
                        for k in range(NTC):
                            v.tensor_copy(
                                out=hstg[xi % 2][:, 3 + k * TCH:3 + (k + 1) * TCH],
                                in_=pA[k][:, 0:TCH])
                        if hh == 0:
                            INC(v.tensor_copy(out=hstg[xi % 2][:, 0:3],
                                              in_=p_h3[:, 0:3]), "evc")
                        else:
                            INC(v.tensor_copy(out=hstg[xi % 2][:, 0:3],
                                              in_=halo_sv[:, xi, :]), "evc")
                        v.drain()
                        if hh == 0:
                            v.tensor_copy(out=halo_sv[:, xi, :],
                                          in_=hstg[xi % 2][:, TOKH:TOKH + 3])
                        if xi >= 1:
                            W(v, "scx", B["scx"] + xi)
                        v.drain()
                        v.tensor_scalar(out=cacc[:], in0=hstg[xi % 2][:, 0:TOKH],
                                        scalar1=convw_s[:, xi, 0:1], scalar2=None,
                                        op0=ALU.mult)
                        for k in range(1, KC):
                            v.drain()
                            v.scalar_tensor_tensor(
                                out=cacc[:], in0=hstg[xi % 2][:, k:TOKH + k],
                                scalar=convw_s[:, xi, k:k + 1], in1=cacc[:],
                                op0=ALU.mult, op1=ALU.add)
                        INC(v.drain(), "cvv")
                    else:
                        v.drain()
                        for k in range(NTC):
                            mm = v.tensor_copy(
                                out=dt_fm[0:64,
                                          hh * TOKH + k * TCH:
                                          hh * TOKH + (k + 1) * TCH],
                                in_=pA[k][0:64, 0:TCH])
                        INC(mm, "evc")
                        INC(v.drain(), "dtq")
                bar(v, "vector")

            @block.scalar
            def _(sc, hh=hh, B=B):
                for ct in range(NCT):
                    if ct < 32:
                        W(sc, "mmr", B["mmr"] + ct + 1)
                        if ct >= 1:
                            W(sc, "spg" if (ct - 1) % 2 == 0 else "spg1",
                              B["spg" if (ct - 1) % 2 == 0 else "spg1"]
                              + 16 * ((ct - 1) // 2 + 1))
                        for k in range(NTC):
                            mm = sc.activation(
                                out=gstg[ct % 2][:, k * TCH:(k + 1) * TCH],
                                in_=pA[k][:, 0:TCH], func=AF.Silu)
                        INC(mm, "evc")
                        INC(sc.drain(), "scg")
                    elif ct < 80:
                        xi = ct - 32
                        W(sc, "cvv", B["cvv"] + xi + 1)
                        if xi >= 1:
                            W(sc, "spx" if (xi - 1) % 2 == 0 else "spx1",
                              B["spx" if (xi - 1) % 2 == 0 else "spx1"]
                              + 16 * ((xi - 1) // 2 + 1))
                        INC(sc.activation(out=cstg[xi % 2][:], in_=cacc[:],
                                          func=AF.Silu,
                                          bias=convb_s[:, xi:xi + 1]), "scx")
                bar(sc, "scalar")

            @block.gpsimd
            def _(gq, hh=hh, B=B):
                for gi in range(32):
                    W(gq, "scg", B["scg"] + gi + 1)
                    INC(gq.dma_start(
                        out=gate_sp[hh * NCHH:(hh + 1) * NCHH, gi]
                        .transpose([1, 0, 2]),
                        in_=gstg[gi % 2][:].rearrange("p (c q) -> p c q",
                                                      c=NCHH)),
                        "spg" if gi % 2 == 0 else "spg1", 16)
                for xi in range(48):
                    W(gq, "scx", B["scx"] + xi + 1)
                    INC(gq.dma_start(
                        out=xbc_sp[hh * NCHH:(hh + 1) * NCHH, xi]
                        .transpose([1, 0, 2]),
                        in_=cstg[xi % 2][:].rearrange("p (c q) -> p c q",
                                                      c=NCHH)),
                        "spx" if xi % 2 == 0 else "spx1", 16)
                bar(gq, "gpsimd")

    if PH <= 1:
        es.close()
        return nc
    with contextlib.ExitStack() as px:
        sbl = lambda n, s, d: px.enter_context(nc.sbuf_tensor(n, s, d))
        spa = sbl("spa", [64, TOK], F32)
        spb = sbl("spb", [64, TOK], F32)

        B = {k: S[k].n for k in S}
        entry_all()

        @block.vector
        def _(v, B=B):
            W(v, "dtq", B["dtq"])
            v.tensor_scalar(out=spb[:], in0=dt_fm[:], scalar1=dtb_s[:],
                            scalar2=None, op0=ALU.add)
            v.drain()
            v.tensor_scalar(out=spa[:], in0=spb[:], scalar1=-1.0,
                            scalar2=None, op0=ALU.mult)
            v.drain()
            INC(v.tensor_tensor(out=spa[:], in0=spa[:], in1=spb[:],
                                op=ALU.max), "dts")
            W(v, "dsv", B["dsv"] + 1)
            v.tensor_tensor(out=dt_fm[:], in0=spb[:], in1=spa[:], op=ALU.add)
            v.drain()
            INC(v.tensor_scalar(out=nega_s[:], in0=nega_s[:], scalar1=-1.0,
                                scalar2=None, op0=ALU.mult), "dts")
            v.drain()
            v.tensor_scalar(out=spa[:], in0=dt_fm[:], scalar1=nega_s[:],
                            scalar2=None, op0=ALU.mult)
            v.drain()
            for c in range(NCH):
                v.tensor_tensor_scan(
                    out=lloc[:, c * QCH:(c + 1) * QCH],
                    data0=spa[:, c * QCH:(c + 1) * QCH],
                    data1=zer128[0:64, 0:QCH], initial=0.0,
                    op0=ALU.add, op1=ALU.add)
            INC(v.drain(), "dts")
            W(v, "dsv", B["dsv"] + 2)
            for c in range(NCH):
                v.tensor_tensor(
                    out=spb[:, c * QCH:(c + 1) * QCH],
                    in0=lloc[:, c * QCH + QCH - 1:c * QCH + QCH]
                    .broadcast_to((64, QCH)),
                    in1=lloc[:, c * QCH:(c + 1) * QCH], op=ALU.subtract)
            INC(v.drain(), "dts")
            v.tensor_copy(out=dect[:], in_=cum_a[:, QCH - 1::QCH])
            INC(v.drain(), "dts")
            for k in range(-(-HCH // 512)):
                wdt = min(512, HCH - k * 512)
                W(v, "pex", B["pex"] + k + 1)
                v.tensor_copy(out=dkbc[:, k * 512:k * 512 + wdt],
                              in_=pA[k % 2][:, 0:wdt])
            INC(v.drain(), "dts")
            bar(v, "vector")

        @block.scalar
        def _(sc, B=B):
            W(sc, "dts", B["dts"] + 1)
            sc.activation(out=spa[:], in_=spa[:], func=AF.Exp, scale=-1.0)
            sc.drain()
            sc.activation(out=spa[:], in_=spa[:], func=AF.Ln, bias=1.0)
            sc.activation(out=spb[:], in_=dt_fm[:], func=AF.Relu, bias=dtb_s[:])
            INC(sc.activation(out=nega_s[:], in_=alog_s[:], func=AF.Exp), "dsv")
            W(sc, "dts", B["dts"] + 3)
            INC(sc.activation(out=cum_a[:], in_=lloc[:], func=AF.Exp), "dsv")
            W(sc, "dts", B["dts"] + 4)
            INC(sc.activation(out=rev_a[:], in_=spb[:], func=AF.Exp), "dsv")
            bar(sc, "scalar")

        @block.sync
        def _(sq, B=B):
            W(sq, "dts", B["dts"] + 3)
            INC(sq.dma_start(out=lfl_d, in_=lloc[:]), "lf0", 16)
            W(sq, "dsv", B["dsv"] + 2)
            INC(sq.dma_start(out=cfl_d, in_=cum_a[:]), "lf0", 16)
            W(sq, "dts", B["dts"] + 5)
            INC(sq.dma_start(out=dkfl[:], in_=dect[:]), "lf0", 16)
            bar(sq, "sync")

        @block.tensor
        def _(te, B=B):
            W(te, "lf0", B["lf0"] + 48)
            for k in range(-(-HCH // 512)):
                wdt = min(512, HCH - k * 512)
                INC(te.matmul(out=pA[k % 2][:, 0:wdt],
                              lhsT=ones1f[0:1, 0:128],
                              rhs=dkfl[0:1, k * 512:k * 512 + wdt],
                              start=True, stop=True), "pex")
            bar(te, "tensor")

        @block.gpsimd
        def _(gq):
            bar(gq, "gpsimd")

    if PH <= 2:
        es.close()
        return nc
    with contextlib.ExitStack() as px:
        sbl = lambda n, s, d: px.enter_context(nc.sbuf_tensor(n, s, d))
        xs_b = sbl("xs_b", [128, NIT, QCH], BF16)
        bc_bt = sbl("bc_bt", [128, 16, QCH], BF16)
        gat_t = sbl("gat_t", [128, NIT, QCH], BF16)
        u_t = sbl("u_t", [128, NIT, QCH], BF16)
        ue_t = sbl("ue_t", [128, NIT, QCH], BF16)
        btm = sbl("btm", [128, G, N], BF16)
        dtT_c = sbl("dtT_c", [128, 64], F32)
        revT_c = sbl("revT_c", [128, 64], F32)
        llT_c = sbl("llT_c", [128, 64], F32)
        scT_c = sbl("scT_c", [128, 64], F32)
        gt_g = sbl("gt_g", [128, 128], BF16)
        dstg = sbl("dstg", [128, HPG * QCH], F32)
        expd = sbl("expd", [128, HPG * QCH], BF16)
        cp_g = sbl("cp_g", [128, HPG, QCH], BF16)
        y_f = sbl("y_f", [128, NIT, QCH], BF16)
        s_b = sbl("s_b", [128, H * 64], BF16)
        sqstg = [sbl(f"sqstg{i}", [128, 128], BF16) for i in range(2)]
        rs_t = sbl("rs_t", [128, 8], F32)
        rs_sd = sbl("rs_sd", [128, 8], F32)
        rs_ft = sbl("rs_ft", [8, 128], BF16)
        gnt_f = sbl("gnt_f", [8, NIT, 128], BF16)
        lcfl = sbl("lcfl", [65, 8 * QCH], F32)
        lflc = [lcfl[0:1, :], lcfl[32:33, :]]
        cflc = [lcfl[64:65, :], lcfl[64:65, :]]

        B = {k: S[k].n for k in S}
        entry_all()

        @block.sync
        def _(sq, B=B):
            for c in range(NCH):
                if c >= 1:
                    W(sq, "tp2", B["tp2"] + c * 40)
                INC(sq.dma_start(out=xs_b[:],
                                 in_=xbc_sp[c, 0:NIT].transpose([1, 0, 2])),
                    "xrl0", 16)
                INC(sq.dma_start(out=bc_bt[:],
                                 in_=xbc_sp[c, NIT:NXT].transpose([1, 0, 2])),
                    "xrl0", 16)
            INC(sq.dma_start(out=gnt_f[:], in_=gnt_in), "grl", 16)
            W(sq, "st2", B["st2"] + 64 * NCH)
            INC(sq.dma_start(out=st_in, in_=s_f[:]), "cq", 16)
            W(sq, "ccs", 4)
            INC(sq.dma_start(out=s_f[:], in_=st_ag[0]), "cq", 16)
            bar(sq, "sync")

        @block.tensor
        def _(te, B=B):
            for c in range(NCH):
                W(te, "xrl0", B["xrl0"] + 32 * (c + 1))
                if c >= 1:
                    W(te, "vb2", B["vb2"] + 10 * (c - 1) + 2)
                INC(te.transpose(out=p_tf[0][:, 0:64],
                                 in_=dt_fm[0:64, c * QCH:(c + 1) * QCH],
                                 identity=id_f[0:64, 0:64]), "tpf")
                INC(te.transpose(out=p_tf[1][:, 0:64],
                                 in_=rev_a[0:64, c * QCH:(c + 1) * QCH],
                                 identity=id_f[0:64, 0:64]), "tpf")
                for j in range(NIT):
                    m = c * NIT + j
                    if j == 0 and c >= 1:
                        W(te, "vb2", B["vb2"] + 10 * (c - 1) + 10)
                    if m >= 2:
                        W(te, "ue2", B["ue2"] + m - 1)
                    INC(te.transpose(out=p_tb[m % 2][:],
                                     in_=xs_b[:, j, :], identity=id_b[:]),
                        "tp2")
                for g in range(G):
                    m = c * G + g
                    if g == 0:
                        W(te, "ue2", B["ue2"] + 32 * (c + 1))
                    if m >= 2:
                        W(te, "vb2", B["vb2"] + 10 * (m // G) + 2 + (m % G) - 1) \
                            if False else W(te, "vb2",
                                            B["vb2"] + 2 * (m // G + 1) + m - 1)
                    INC(te.transpose(out=p_tc[m % 2][:],
                                     in_=bc_bt[:, g, :], identity=id_b[:]),
                        "tp2")
                for j in range(NIT - G):
                    INC(te.sem_inc(S["tp2"].h, 0), "tp2") if False else None
                S["tp2"].n += 0
                for h in range(H):
                    g, it = h // HPG, h // 2
                    W(te, "ue2", B["ue2"] + c * NIT + it + 1)
                    W(te, "vb2", B["vb2"] + c * 10 + 2 + g + 1)
                    if h >= 2:
                        W(te, "st2", B["st2"] + c * H + h - 1)
                    INC(te.matmul(out=p_x[h % 2][:], lhsT=btm[:, g],
                                  rhs=ue_t[:, it, (h % 2) * 64:(h % 2 + 1) * 64],
                                  start=True, stop=True), "pex")
            bar(te, "tensor")

        @block.vector
        def _(v, B=B):
            for c in range(NCH):
                W(v, "tpf", B["tpf"] + 2 * c + 2)
                v.tensor_copy(out=dtT_c[:], in_=p_tf[0][:, 0:64])
                INC(v.tensor_copy(out=revT_c[:], in_=p_tf[1][:, 0:64]), "vb2")
                v.drain()
                INC(v.tensor_tensor(out=scT_c[:], in0=dtT_c[:], in1=revT_c[:],
                                    op=ALU.mult), "vb2")
                for g in range(G):
                    m = c * G + g
                    W(v, "tp2", B["tp2"] + c * 40 + NIT + g + 1)
                    INC(v.tensor_copy(out=btm[:, g], in_=p_tc[m % 2][:]), "vb2")
                for h in range(H):
                    W(v, "pex", B["pex"] + c * H + h + 1)
                    INC(v.scalar_tensor_tensor(
                        out=s_f[:, h * 64:(h + 1) * 64],
                        in0=s_f[:, h * 64:(h + 1) * 64],
                        scalar=dkbc[:, h * NCH + c:h * NCH + c + 1],
                        in1=p_x[h % 2][:], op0=ALU.mult, op1=ALU.add), "st2")
                v.drain()
            W(v, "cq", B["cq"] + 32)
            INC(v.tensor_scalar(out=s_f[:], in0=s_f[:], scalar1=ssc_s[:],
                                scalar2=None, op0=ALU.mult), "sti")
            v.drain()
            bar(v, "vector")

        @block.scalar
        def _(sc, B=B):
            for c in range(NCH):
                W(sc, "vb2", B["vb2"] + c * 10 + 2)
                for j in range(NIT):
                    W(sc, "tp2", B["tp2"] + c * 40 + j + 1)
                    sc.activation(out=ue_t[:, j, 0:64], in_=p_tb[(c * NIT + j) % 2][:, 0:64],
                                  func=AF.Identity, scale=scT_c[:, 2 * j:2 * j + 1])
                    INC(sc.activation(out=ue_t[:, j, 64:128],
                                      in_=p_tb[(c * NIT + j) % 2][:, 64:128],
                                      func=AF.Identity,
                                      scale=scT_c[:, 2 * j + 1:2 * j + 2]), "ue2")
            bar(sc, "scalar")

        @block.gpsimd
        def _(gq, B=B):
            W(gq, "cq", B["cq"] + 16)
            INC(gq.collective_compute(
                "AllGather", ALU.bypass,
                replica_groups=[[i, i + 1] for i in range(0, NCORES, 2)],
                ins=[st_in[:]], outs=[st_ag[:]]), "ccs")
            bar(gq, "gpsimd")

        if KP2 >= 2:
            for nm in ("gmm", "lbm", "cum", "gnc"):
                S[nm] = Sem(nm)
            B = {k: S[k].n for k in S}
            entry_all()

            @block.sync
            def _(sq, B=B):
                for c in range(NCH):
                    if c >= 1:
                        W(sq, "tp2", B["tp2"] + 40 * c)
                        W(sq, "yev", B["yev"] + 32 * c)
                        W(sq, "cpq", B["cpq"] + 64 * c)
                        W(sq, "ynd", B["ynd"] + 32 * c)
                    INC(sq.dma_start(out=xs_b[:],
                                     in_=xbc_sp[c, 0:NIT].transpose([1, 0, 2])),
                        "xrl1", 16)
                    INC(sq.dma_start(out=bc_bt[:],
                                     in_=xbc_sp[c, NIT:NXT].transpose([1, 0, 2])),
                        "xrl1", 16)
                    INC(sq.dma_start(out=gat_t[:],
                                     in_=gate_sp[c].transpose([1, 0, 2])),
                        "xrl1", 16)
                    for g in range(G):
                        m = c * G + g
                        if m >= 2:
                            W(sq, "lbm", B["lbm"] + 64 * ((m - 2) // G)
                              + 8 * ((m - 2) % G) + 8)
                        if m >= 1:
                            W(sq, "cum", B["cum"] + 64 * ((m - 1) // G)
                              + 8 * ((m - 1) % G) + 8)
                        INC(sq.dma_start(
                            out=lflc[g % 2][:],
                            in_=lfl_d[8 * g:8 * g + 8, c * QCH:(c + 1) * QCH]),
                            "gld0" if g % 2 == 0 else "gld1", 16)
                        INC(sq.dma_start(
                            out=cflc[g % 2][:],
                            in_=cfl_d[8 * g:8 * g + 8, c * QCH:(c + 1) * QCH]),
                            "gld0" if g % 2 == 0 else "gld1", 16)
                bar(sq, "sync")

            @block.tensor
            def _(te, B=B):
                W(te, "sti", B["sti"])
                for c in range(NCH):
                    W(te, "xrl1", B["xrl1"] + 48 * (c + 1))
                    if c >= 1:
                        W(te, "vb2", B["vb2"] + 13 * (c - 1) + 13)
                        W(te, "ynd", B["ynd"] + 32 * c)
                    INC(te.transpose(out=p_tf[0][:, 0:64],
                                     in_=dt_fm[0:64, c * QCH:(c + 1) * QCH],
                                     identity=id_f[0:64, 0:64]), "tpf")
                    W(te, "vb2", B["vb2"] + 13 * c + 1)
                    INC(te.transpose(out=p_tf[1][:, 0:64],
                                     in_=rev_a[0:64, c * QCH:(c + 1) * QCH],
                                     identity=id_f[0:64, 0:64]), "tpf")
                    W(te, "vb2", B["vb2"] + 13 * c + 2)
                    INC(te.transpose(out=p_tf[0][:, 0:64],
                                     in_=lloc[0:64, c * QCH:(c + 1) * QCH],
                                     identity=id_f[0:64, 0:64]), "tpf")
                    for j in range(NIT):
                        m = c * NIT + j
                        if j == 0 and c >= 1:
                            W(te, "vb2", B["vb2"] + 13 * (c - 1) + 12)
                        if m >= 2:
                            W(te, "ue2", B["ue2"] + m - 1)
                            W(te, "ut2", B["ut2"] + m - 1)
                        INC(te.transpose(out=p_tb[m % 2][:], in_=xs_b[:, j, :],
                                         identity=id_b[:]), "tp2")
                    for g in range(G):
                        m = c * G + g
                        if g == 0:
                            W(te, "ue2", B["ue2"] + 32 * (c + 1))
                            W(te, "ut2", B["ut2"] + 32 * (c + 1))
                        if m >= 2:
                            cc2, gg2 = divmod(m - 2, G)
                            W(te, "vb2", B["vb2"] + 13 * cc2 + 5 + gg2)
                        INC(te.transpose(out=p_tc[m % 2][:], in_=bc_bt[:, g, :],
                                         identity=id_b[:]), "tp2")
                    for g in range(G):
                        W(te, "gev", B["gev"] + 8 * c + g)
                        W(te, "yev", B["yev"] + 32 * c + 4 * g)
                        INC(te.matmul(out=p_g[:], lhsT=bc_bt[:, g, :],
                                      rhs=bc_bt[:, 8 + g, :],
                                      start=True, stop=True), "gmm")
                        m = c * G + g
                        W(te, "gld0" if g % 2 == 0 else "gld1",
                          B["gld0" if g % 2 == 0 else "gld1"] + 32 * (m // 2 + 1))
                        for j in range(G):
                            l = 64 * c + 8 * g + j
                            if j == 0:
                                W(te, "cpq", B["cpq"] + 64 * c + 8 * g)
                                W(te, "st2", B["st2"] + 64 * c + 8 * g)
                            if l >= 2:
                                W(te, "dsv", B["dsv"] + l - 1)
                            lbase = 0 if g % 2 == 0 else 32
                            INC(te.matmul(
                                out=p_lb[j % 2][:],
                                lhsT=ones65[lbase:lbase + 1, :],
                                rhs=lflc[g % 2][0:1, j * QCH:(j + 1) * QCH],
                                start=True, stop=True), "lbm")
                        for j in range(G):
                            l = 64 * c + 8 * g + j
                            if j == 0:
                                W(te, "dsv", B["dsv"] + 64 * c + 8 * g + 8)
                            if l >= 2:
                                W(te, "cpq", B["cpq"] + l - 1)
                            INC(te.matmul(
                                out=p_cu[j % 2][:],
                                lhsT=ones65[64:65, :],
                                rhs=cflc[g % 2][0:1, j * QCH:(j + 1) * QCH],
                                start=True, stop=True), "cum")
                        W(te, "mtq", B["mtq"] + 8 * c + g + 1)
                        W(te, "sbc", B["sbc"] + c + 1)
                        for j2 in range(4):
                            it = 4 * g + j2
                            W(te, "cpq", B["cpq"] + 64 * c + 8 * g + 2 * j2 + 2)
                            W(te, "ut2", B["ut2"] + 32 * c + it + 1)
                            if 32 * c + it >= 1:
                                W(te, "yev", B["yev"] + 32 * c + it - 1)
                            for hh2 in range(2):
                                h = 8 * g + 2 * j2 + hh2
                                r0, r1 = hh2 * 64, hh2 * 64 + 64
                                if "2" in KNEU:
                                    r0, r1 = 0, 64
                                te.matmul(out=p_y[it % 2][r0:r1, :],
                                          lhsT=u_t[:, it, r0:r1],
                                          rhs=expd[:, (2 * j2 + hh2) * QCH:
                                                   (2 * j2 + hh2 + 1) * QCH],
                                          start=True, stop=False)
                                mm = te.matmul(out=p_y[it % 2][r0:r1, :],
                                               lhsT=s_b[:, h * 64:(h + 1) * 64],
                                               rhs=cp_g[:, 2 * j2 + hh2, :],
                                               start=False, stop=True)
                            INC(mm, "ymd")
                        for j in range(G):
                            h = 8 * g + j
                            it = h // 2
                            W(te, "ue2", B["ue2"] + 32 * c + it + 1)
                            l = 64 * c + 8 * g + j
                            if l >= 2:
                                W(te, "st2", B["st2"] + l - 1)
                            INC(te.matmul(
                                out=p_x[j % 2][:], lhsT=btm[:, g],
                                rhs=ue_t[:, it, (h % 2) * 64:(h % 2) * 64 + 64],
                                start=True, stop=True), "pex")
                    for it in range(NIT):
                        if it == 0:
                            W(te, "yev", B["yev"] + 32 * (c + 1))
                        W(te, "sqq", B["sqq"] + 32 * c + it + 1)
                        mm = te.matmul(out=p_gn[:], lhsT=sqstg[it % 2][:],
                                       rhs=gne_b[:, it],
                                       start=(it == 0) or "1" in KNEU,
                                       stop=(it == NIT - 1) or "1" in KNEU)
                        INC(mm, "gnc")
                    W(te, "rsq", B["rsq"] + 3 * c + 3)
                    W(te, "vb2", B["vb2"] + 13 * c + 3)
                    INC(te.transpose(out=p_tf[0][0:8, 0:128], in_=rs_t[:],
                                     identity=id_f[:]), "tpf")
                    W(te, "vb2", B["vb2"] + 13 * c + 13)
                    if c == 0:
                        W(te, "grl", 16)
                    for it in range(NIT):
                        if 32 * c + it >= 2:
                            W(te, "ynd", B["ynd"] + 32 * c + it - 1)
                        INC(te.matmul(out=p_rs[it % 2][:],
                                      lhsT=(rs_ft[:] if "3" in KNEU
                                            else gnt_f[:, it]),
                                      rhs=rs_ft[:], start=True, stop=True), "rsm")
                bar(te, "tensor")

            @block.vector
            def _(v, B=B):
                for c in range(NCH):
                    if c >= 1:
                        v.drain()
                    W(v, "sqq", B["sqq"] + 32 * c)
                    W(v, "spy", B["spy"] + 16 * c)
                    INC(v.tensor_copy(out=s_b[:], in_=s_f[:]), "sbc")
                    W(v, "tpf", B["tpf"] + 4 * c + 1)
                    INC(v.tensor_copy(out=dtT_c[:], in_=(
                        scT_c[:] if "a" in KNR else p_tf[0][:, 0:64])), "vb2")
                    W(v, "tpf", B["tpf"] + 4 * c + 2)
                    INC(v.tensor_copy(out=revT_c[:], in_=(
                        scT_c[:] if "a" in KNR else p_tf[1][:, 0:64])), "vb2")
                    W(v, "tpf", B["tpf"] + 4 * c + 3)
                    INC(v.tensor_copy(out=llT_c[:], in_=(
                        scT_c[:] if "a" in KNR else p_tf[0][:, 0:64])), "vb2")
                    v.drain()
                    INC(v.tensor_tensor(out=scT_c[:], in0=dtT_c[:], in1=revT_c[:],
                                        op=ALU.mult), "vb2")
                    for j in range(NIT):
                        W(v, "tp2", B["tp2"] + 40 * c + j + 1)
                        ptbj = (xs_b[:, j, :] if "b" in KNR
                                else p_tb[(c * NIT + j) % 2][:])
                        v.tensor_scalar(out=u_t[:, j, 0:64],
                                        in0=ptbj[:, 0:64],
                                        scalar1=dtT_c[:, 2 * j:2 * j + 1],
                                        scalar2=None, op0=ALU.mult)
                        INC(v.tensor_scalar(out=u_t[:, j, 64:128],
                                            in0=ptbj[:, 64:128],
                                            scalar1=dtT_c[:, 2 * j + 1:2 * j + 2],
                                            scalar2=None, op0=ALU.mult), "ut2")
                    for g in range(G):
                        W(v, "tp2", B["tp2"] + 40 * c + NIT + g + 1)
                        INC(v.tensor_copy(out=btm[:, g], in_=(
                            bc_bt[:, g, :] if "c" in KNR
                            else p_tc[(c * G + g) % 2][:])), "vb2")
                    for g in range(G):
                        W(v, "gmm", B["gmm"] + 8 * c + g + 1)
                        INC(v.tensor_tensor(out=gt_g[:], in0=(
                            dstg[:, 0:128] if "d" in KNR else p_g[:]),
                            in1=tri_b[:], op=ALU.mult), "gev")
                        for j in range(G):
                            l = 64 * c + 8 * g + j
                            W(v, "lbm", B["lbm"] + l + 1)
                            INC(v.scalar_tensor_tensor(
                                out=dstg[:, j * QCH:(j + 1) * QCH],
                                in0=(zer128[:] if "e" in KNR
                                     else p_lb[j % 2][:]),
                                scalar=llT_c[:, 8 * g + j:8 * g + j + 1],
                                in1=zer128[:], op0=ALU.subtract, op1=ALU.min),
                                "dsv")
                        W(v, "exq", B["exq"] + 8 * c + g + 1)
                        INC(v.tensor_tensor(
                            out=expd[:], in0=expd[:],
                            in1=gt_g[:].unsqueeze(1).broadcast_to((128, G, QCH)),
                            op=ALU.mult), "mtq")
                        if 8 * c + g >= 1:
                            W(v, "ymd", B["ymd"] + 32 * c + 4 * g)
                        for j in range(G):
                            l = 64 * c + 8 * g + j
                            W(v, "cum", B["cum"] + l + 1)
                            INC(v.tensor_tensor(out=cp_g[:, j, :],
                                                in0=bc_bt[:, 8 + g, :],
                                                in1=(zer128[:] if "f" in KNR
                                                     else p_cu[j % 2][:]),
                                                op=ALU.mult), "cpq")
                        for j2 in range(4):
                            it = 4 * g + j2
                            W(v, "ymd", B["ymd"] + 32 * c + it + 1)
                            INC(v.scalar_tensor_tensor(
                                out=y_f[:, it, :], in0=xs_b[:, it, :],
                                scalar=dtl_s[:, it:it + 1],
                                in1=(zer128[:] if "g" in KNR
                                     else p_y[it % 2][:]),
                                op0=ALU.mult, op1=ALU.add), "yev")
                        for j in range(G):
                            h = 8 * g + j
                            l = 64 * c + 8 * g + j
                            W(v, "pex", B["pex"] + l + 1)
                            INC(v.scalar_tensor_tensor(
                                out=s_f[:, h * 64:(h + 1) * 64],
                                in0=s_f[:, h * 64:(h + 1) * 64],
                                scalar=dkbc[:, h * NCH + c:h * NCH + c + 1],
                                in1=(zer128[:, 0:64] if "h" in KNR
                                     else p_x[j % 2][:]),
                                op0=ALU.mult, op1=ALU.add),
                                "st2")
                    W(v, "gnc", B["gnc"] + 32 * (c + 1))
                    INC(v.tensor_scalar(out=rs_t[:], in0=(
                        rs_sd[:] if "i" in KNR else p_gn[:]),
                                        scalar1=1.0 / 512, scalar2=EPS,
                                        op0=ALU.mult, op1=ALU.add), "rsq")
                    W(v, "rsq", B["rsq"] + 3 * c + 2)
                    v.reciprocal(out=rs_t[:], in_=rs_sd[:])
                    INC(v.drain(), "rsq")
                    W(v, "tpf", B["tpf"] + 4 * c + 4)
                    INC(v.tensor_copy(out=rs_ft[:], in_=(
                        gnt_f[:, 0] if "a" in KNR
                        else p_tf[0][0:8, 0:128])), "vb2")
                    for it in range(NIT):
                        W(v, "rsm", B["rsm"] + 32 * c + it + 1)
                        v.tensor_tensor(out=y_f[:, it, :], in0=y_f[:, it, :],
                                        in1=gat_t[:, it, :], op=ALU.mult)
                        v.drain()
                        INC(v.scalar_tensor_tensor(
                            out=y_f[:, it, :], in0=y_f[:, it, :],
                            scalar=nw_s[:, it:it + 1],
                            in1=(zer128[:] if "j" in KNR
                                 else p_rs[it % 2][:]),
                            op0=ALU.mult, op1=ALU.mult), "ynd")
                bar(v, "vector")

            @block.scalar
            def _(sc, B=B):
                for c in range(NCH):
                    W(sc, "vb2", B["vb2"] + 13 * c + 4)
                    for j in range(NIT):
                        W(sc, "ut2", B["ut2"] + 32 * c + j + 1)
                        sc.activation(out=ue_t[:, j, 0:64],
                                      in_=u_t[:, j, 0:64],
                                      func=AF.Identity,
                                      scale=revT_c[:, 2 * j:2 * j + 1])
                        INC(sc.activation(out=ue_t[:, j, 64:128],
                                          in_=u_t[:, j, 64:128],
                                          func=AF.Identity,
                                          scale=revT_c[:, 2 * j + 1:2 * j + 2]),
                            "ue2")
                    for g in range(G):
                        W(sc, "dsv", B["dsv"] + 64 * c + 8 * g + 8)
                        if 8 * c + g >= 1:
                            W(sc, "ymd", B["ymd"] + 32 * c + 4 * g)
                        INC(sc.activation(out=expd[:], in_=dstg[:], func=AF.Exp),
                            "exq")
                    for it in range(NIT):
                        W(sc, "yev", B["yev"] + 32 * c + it + 1)
                        m = 32 * c + it
                        if m >= 2:
                            W(sc, "gnc", B["gnc"] + m - 1)
                        INC(sc.activation(out=sqstg[it % 2][:], in_=y_f[:, it, :],
                                          func=AF.Square), "sqq")
                    W(sc, "rsq", B["rsq"] + 3 * c + 1)
                    INC(sc.activation(out=rs_sd[:], in_=rs_t[:], func=AF.Sqrt),
                        "rsq")
                bar(sc, "scalar")

            @block.gpsimd
            def _(gq, B=B):
                for c in range(NCH):
                    W(gq, "ynd", B["ynd"] + 32 * (c + 1))
                    INC(gq.dma_start(out=yn_sp[c].transpose([1, 0, 2]),
                                     in_=y_f[:]), "spy", 16)
                bar(gq, "gpsimd")

    def ln_v(v, n, src, Blnv, Blns):
        v.tensor_reduce(out=st1[:, 0:1], in_=src, axis=AX.X, op=ALU.add)
        INC(v.drain(), "lnv")
        W(v, "lns", Blns + 2 * n + 1)
        v.tensor_scalar(out=st1[:, 2:3], in0=st1[:, 0:1], scalar1=1.0 / DM,
                        scalar2=None, op0=ALU.mult)
        v.drain()
        v.tensor_tensor(out=st1[:, 3:4], in0=st1[:, 2:3], in1=st1[:, 2:3],
                        op=ALU.mult)
        v.tensor_scalar(out=st1[:, 4:5], in0=st1[:, 1:2], scalar1=1.0 / DM,
                        scalar2=None, op0=ALU.mult)
        v.drain()
        v.tensor_tensor(out=st1[:, 5:6], in0=st1[:, 4:5], in1=st1[:, 3:4],
                        op=ALU.subtract)
        v.drain()
        v.tensor_scalar(out=st1[:, 6:7], in0=st1[:, 5:6], scalar1=EPS,
                        scalar2=None, op0=ALU.add)
        INC(v.drain(), "lnv")
        W(v, "lns", Blns + 2 * n + 2)
        v.reciprocal(out=st1[:, 8:9], in_=st1[:, 7:8])
        v.drain()
        v.tensor_scalar(out=st1[:, 9:10], in0=st1[:, 2:3], scalar1=st1[:, 8:9],
                        scalar2=-1.0, op0=ALU.mult, op1=ALU.mult)
        INC(v.drain(), "lnv")

    def ln_sc(sc, n, src, dst, sqbuf, Blnv, Blns):
        W(sc, "lnv", Blnv + 3 * n + 1)
        INC(sc.activation(out=sqbuf, in_=src, func=AF.Square,
                          accum_out=st1[:, 1:2]), "lns")
        W(sc, "lnv", Blnv + 3 * n + 2)
        INC(sc.activation(out=st1[:, 7:8], in_=st1[:, 6:7], func=AF.Sqrt),
            "lns")
        W(sc, "lnv", Blnv + 3 * n + 3)
        INC(sc.activation(out=dst, in_=src, func=AF.Identity, scale=st1[:, 8:9],
                          bias=st1[:, 9:10]), "lna")

    if PH <= 3:
        es.close()
        return nc
    with contextlib.ExitStack() as px:
        sbl = lambda n, s, d: px.enter_context(nc.sbuf_tensor(n, s, d))
        wosl = [sbl(f"wosl{i}", [128, NIT, 512], BF16) for i in range(2)]
        ynld = [sbl(f"ynld{i}", [128, NIT, 128], BF16) for i in range(2)]
        mixst = [sbl(f"mixst{i}", [128, 512], BF16) for i in range(2)]

        B = {k: S[k].n for k in S}
        entry_all()

        @block.sync
        def _(sq, B=B):
            for dc in range(8):
                if dc >= 2:
                    W(sq, "opd", B["opd"] + NTT * (dc - 1))
                src = bass.AP(woT_ag.tensor, woT_ag.offset + dc * 512,
                              [[DM, 128], [128 * DM, NIT], [1, 512]])
                INC(sq.dma_start(out=wosl[dc % 2][:], in_=src),
                    "wol0" if dc % 2 == 0 else "wol1", 16)
                for tt in range(NTT):
                    n = dc * NTT + tt
                    if n >= 2:
                        W(sq, "opd", B["opd"] + n - 1)
                    INC(sq.dma_start(out=ynld[n % 2][:],
                                     in_=yn_sp[tt].transpose([1, 0, 2])),
                        "ynl0" if n % 2 == 0 else "ynl1", 16)
            bar(sq, "sync")

        @block.tensor
        def _(te, B=B):
            W(te, "ccs", 2)
            for dc in range(8):
                W(te, "wol0" if dc % 2 == 0 else "wol1",
                  B["wol0" if dc % 2 == 0 else "wol1"] + 16 * (dc // 2 + 1))
                for tt in range(NTT):
                    n = dc * NTT + tt
                    W(te, "ynl0" if n % 2 == 0 else "ynl1",
                      B["ynl0" if n % 2 == 0 else "ynl1"] + 16 * (n // 2 + 1))
                    if n >= 2:
                        W(te, "mxs", B["mxs"] + n - 1)
                    for it in range(NIT):
                        mm = te.matmul(out=pA[n % 2][:, 0:512],
                                       lhsT=ynld[n % 2][:, it, :],
                                       rhs=wosl[dc % 2][:, it, :],
                                       start=(it == 0), stop=(it == NIT - 1))
                    INC(mm, "opd")
            bar(te, "tensor")

        @block.vector
        def _(v, B=B):
            for n in range(8 * NTT):
                W(v, "opd", B["opd"] + n + 1)
                if n >= 2:
                    W(v, "spm" if n % 2 == 0 else "spm1",
                      B["spm" if n % 2 == 0 else "spm1"] + 16 * (n // 2))
                INC(v.tensor_copy(out=mixst[n % 2][:], in_=pA[n % 2][:, 0:512]),
                    "mxs")
            bar(v, "vector")

        @block.gpsimd
        def _(gq, B=B):
            for dc in range(8):
                for tt in range(NTT):
                    n = dc * NTT + tt
                    W(gq, "mxs", B["mxs"] + n + 1)
                    INC(gq.dma_start(out=mix_sp[tt][:, dc * 512:(dc + 1) * 512],
                                     in_=mixst[n % 2][:]),
                        "spm" if n % 2 == 0 else "spm1", 16)
            bar(gq, "gpsimd")

        @block.scalar
        def _(sc):
            bar(sc, "scalar")

    if PH <= 4:
        es.close()
        return nc
    with contextlib.ExitStack() as px:
        sbl = lambda n, s, d: px.enter_context(nc.sbuf_tensor(n, s, d))
        mix_t = sbl("mix_t", [128, DM], BF16)
        xt2 = [sbl("xt2s", [128, DM], F32)] * 2
        lnm_t = sbl("lnm_t", [128, DM], F32)
        x2_t = sbl("x2_t", [128, DM], F32)
        xn2_b = sbl("xn2_b", [128, DM], BF16)
        sq_b4 = sbl("sq_b4", [128, DM], BF16)
        xfstg = sbl("xfstg", [128, NDT, 128], BF16)

        B = {k: S[k].n for k in S}
        entry_all()

        @block.sync
        def _(sq, B=B):
            for tt in range(NTT):
                if tt >= 1:
                    W(sq, "lna", B["lna"] + 2 * tt)
                    W(sq, "x2s", B["x2s"] + tt)
                INC(sq.dma_start(out=mix_t[:], in_=mix_sp[tt]),
                    "xld0" if tt % 2 == 0 else "xld1", 16)
                INC(sq.dma_start(out=xt2[tt % 2][:],
                                 in_=x_in[tt * 128:(tt + 1) * 128]),
                    "xld0" if tt % 2 == 0 else "xld1", 16)
            bar(sq, "sync")

        @block.vector
        def _(v, B=B):
            for tt in range(NTT):
                W(v, "xld0" if tt % 2 == 0 else "xld1",
                  B["xld0" if tt % 2 == 0 else "xld1"] + 32 * (tt // 2 + 1))
                ln_v(v, 2 * tt, mix_t[:], B["lnv"], B["lns"])
                W(v, "lna", B["lna"] + 2 * tt + 1)
                if tt >= 1:
                    W(v, "xfs", B["xfs"] + 16 * tt)
                v.tensor_tensor(out=x2_t[:], in0=xt2[tt % 2][:], in1=lnm_t[:],
                                op=ALU.add)
                INC(v.drain(), "x2s")
                ln_v(v, 2 * tt + 1, x2_t[:], B["lnv"], B["lns"])
                if tt >= 1:
                    W(v, "xfs2", B["xfs2"] + 16 * tt)
                for dt in range(NDT):
                    W(v, "tp", B["tp"] + tt * NDT + dt + 1)
                    INC(v.tensor_copy(out=xfstg[:, dt, :],
                                      in_=p_tb[(tt * NDT + dt) % 2][:]), "ev")
            bar(v, "vector")

        @block.scalar
        def _(sc, B=B):
            for tt in range(NTT):
                ln_sc(sc, 2 * tt, mix_t[:], lnm_t[:], sq_b4[:], B["lnv"],
                      B["lns"])
                W(sc, "x2s", B["x2s"] + tt + 1)
                if tt >= 1:
                    W(sc, "tp", B["tp"] + tt * NDT)
                ln_sc(sc, 2 * tt + 1, x2_t[:], xn2_b[:], sq_b4[:], B["lnv"],
                      B["lns"])
            bar(sc, "scalar")

        @block.tensor
        def _(te, B=B):
            for tt in range(NTT):
                W(te, "lna", B["lna"] + 2 * tt + 2)
                for dt in range(NDT):
                    n = tt * NDT + dt
                    if n >= 2:
                        W(te, "ev", B["ev"] + n - 1)
                    INC(te.transpose(out=p_tb[n % 2][:],
                                     in_=xn2_b[:, dt * 128:(dt + 1) * 128],
                                     identity=id_b[:]), "tp")
            bar(te, "tensor")

        @block.gpsimd
        def _(gq, B=B):
            for tt in range(NTT):
                W(gq, "x2s", B["x2s"] + tt + 1)
                INC(gq.dma_start(out=x2_sp[tt], in_=x2_t[:]), "xfs", 16)
                W(gq, "ev", B["ev"] + 32 * (tt + 1))
                INC(gq.dma_start(out=xf_sp[tt].transpose([1, 0, 2]),
                                 in_=xfstg[:]), "xfs2", 16)
            bar(gq, "gpsimd")

    if PH <= 5:
        es.close()
        return nc
    with contextlib.ExitStack() as px:
        sbl = lambda n, s, d: px.enter_context(nc.sbuf_tensor(n, s, d))
        wfsl = [sbl("wfsl0", [128, NDT, 512], BF16)] * 2
        xfld = [sbl(f"xfld{i}", [128, NDT, 128], BF16) for i in range(2)]
        yfst = [sbl(f"yfst{i}", [128, 512], BF16) for i in range(2)]
        fb_s = sbl("fb_s", [1, DM], F32)
        fbb = sbl("fbb", [128, 512], F32)

        B = {k: S[k].n for k in S}
        entry_all()

        @block.tensor
        def _(te, B=B):
            W(te, "ccs", 3)
            W(te, "grl", 32)
            for oc in range(8):
                if oc >= 1:
                    W(te, "mxs", B["mxs"] + NTT * oc)
                INC(te.matmul(out=pb8[:, 0:512], lhsT=ones1f[0:1, 0:128],
                              rhs=fb_s[0:1, oc * 512:(oc + 1) * 512],
                              start=True, stop=True), "opd")
                W(te, "ffl0" if oc % 2 == 0 else "ffl1",
                  B["ffl0" if oc % 2 == 0 else "ffl1"] + 16 * (oc // 2 + 1))
                for tt in range(NTT):
                    n = oc * NTT + tt
                    W(te, "yfl" if n % 2 == 0 else "yfl1",
                      B["yfl" if n % 2 == 0 else "yfl1"] + 16 * (n // 2 + 1))
                    if n >= 2:
                        W(te, "mxs", B["mxs"] + n - 1)
                    for it in range(NDT):
                        mm = te.matmul(out=pA[n % 2][:, 0:512],
                                       lhsT=xfld[n % 2][:, it, :],
                                       rhs=wfsl[oc % 2][:, it, :],
                                       start=(it == 0), stop=(it == NDT - 1))
                    INC(mm, "ffd")
            bar(te, "tensor")

        @block.sync
        def _(sq, B=B):
            INC(sq.dma_start(out=fb_s[:], in_=fb_in), "grl", 16)
            for oc in range(8):
                if oc >= 1:
                    W(sq, "ffd", B["ffd"] + NTT * oc)
                cidx, off = (oc * 512) // OPC, (oc * 512) % OPC
                src = bass.AP(wlT_ag.tensor,
                              wlT_ag.offset + cidx * NIT * 128 * OPC + off,
                              [[OPC, 128], [128 * OPC, NDT], [1, 512]])
                INC(sq.dma_start(out=wfsl[oc % 2][:], in_=src),
                    "ffl0" if oc % 2 == 0 else "ffl1", 16)
                for tt in range(NTT):
                    n = oc * NTT + tt
                    if n >= 2:
                        W(sq, "ffd", B["ffd"] + n - 1)
                    INC(sq.dma_start(out=xfld[n % 2][:],
                                     in_=xf_sp[tt].transpose([1, 0, 2])),
                        "yfl" if n % 2 == 0 else "yfl1", 16)
            bar(sq, "sync")

        @block.vector
        def _(v, B=B):
            for n in range(8 * NTT):
                W(v, "ffd", B["ffd"] + n + 1)
                if n % NTT == 0:
                    W(v, "fbc", B["fbc"] + n // NTT + 1)
                if n >= 2:
                    W(v, "spm" if n % 2 == 0 else "spm1",
                      B["spm" if n % 2 == 0 else "spm1"] + 16 * (n // 2))
                INC(v.tensor_tensor(out=yfst[n % 2][:], in0=pA[n % 2][:, 0:512],
                                    in1=fbb[:], op=ALU.add), "mxs")
            bar(v, "vector")

        @block.gpsimd
        def _(gq, B=B):
            for oc in range(8):
                for tt in range(NTT):
                    n = oc * NTT + tt
                    W(gq, "mxs", B["mxs"] + n + 1)
                    INC(gq.dma_start(out=yf_sp[tt][:, oc * 512:(oc + 1) * 512],
                                     in_=yfst[n % 2][:]),
                        "spm" if n % 2 == 0 else "spm1", 16)
            bar(gq, "gpsimd")

        @block.scalar
        def _(sc, B=B):
            for oc in range(8):
                W(sc, "opd", B["opd"] + oc + 1)
                INC(sc.activation(out=fbb[:], in_=pb8[:, 0:512],
                                  func=AF.Copy), "fbc")
            bar(sc, "scalar")

    if PH <= 6:
        es.close()
        return nc
    with contextlib.ExitStack() as px:
        sbl = lambda n, s, d: px.enter_context(nc.sbuf_tensor(n, s, d))
        yff_t = [sbl(f"yff_t{i}", [128, DM], BF16) for i in range(2)]
        x2b_t = sbl("x2b_t", [128, DM], F32)
        out_t = [sbl("out_ts", [128, DM], F32)] * 2
        lnm6 = sbl("lnm6", [128, DM], F32)
        sq6 = sbl("sq6", [128, DM], BF16)

        B = {k: S[k].n for k in S}
        entry_all()

        @block.sync
        def _(sq, B=B):
            for tt in range(NTT):
                if tt >= 2:
                    W(sq, "lna", B["lna"] + tt - 1)
                INC(sq.dma_start(out=yff_t[tt % 2][:], in_=yf_sp[tt]),
                    "xld0" if tt % 2 == 0 else "xld1", 16)
                if tt >= 1:
                    W(sq, "x2s", B["x2s"] + tt)
                INC(sq.dma_start(out=x2b_t[:], in_=x2_sp[tt]),
                    "xld0" if tt % 2 == 0 else "xld1", 16)
            W(sq, "od0", B["od0"] + 16 * NTT)

        @block.vector
        def _(v, B=B):
            for tt in range(NTT):
                W(v, "xld0" if tt % 2 == 0 else "xld1",
                  B["xld0" if tt % 2 == 0 else "xld1"] + 32 * (tt // 2 + 1))
                ln_v(v, tt, yff_t[tt % 2][:], B["lnv"], B["lns"])
                W(v, "lna", B["lna"] + tt + 1)
                if tt >= 1:
                    W(v, "od0", B["od0"] + 16 * tt)
                v.tensor_tensor(out=out_t[tt % 2][:], in0=x2b_t[:],
                                in1=lnm6[:], op=ALU.add)
                INC(v.drain(), "x2s")

        @block.scalar
        def _(sc, B=B):
            for tt in range(NTT):
                ln_sc(sc, tt, yff_t[tt % 2][:], lnm6[:], sq6[:], B["lnv"],
                      B["lns"])

        @block.gpsimd
        def _(gq, B=B):
            for tt in range(NTT):
                W(gq, "x2s", B["x2s"] + tt + 1)
                INC(gq.dma_start(out=out[tt * 128:(tt + 1) * 128],
                                 in_=out_t[tt % 2][:]), "od0", 16)

        @block.tensor
        def _(te):
            pass

    es.close()
    return nc


def kernel(**inputs):
    TOK, NCORES = 2048, 8
    in_maps = _host_prepare(inputs, TOK, NCORES)
    nc = build_program(TOK, NCORES)
    res = run_bass_kernel_spmd(nc, in_maps, core_ids=list(range(NCORES)))
    Bsz, L, _ = inputs["L0_attn_hidden_states"].shape
    o = np.zeros((Bsz, L, DM), np.float32)
    for c in range(NCORES):
        b, hh = divmod(c, 2)
        o[b, hh * TOK:(hh + 1) * TOK] = res.results[c]["out"]
    return o
